# version 18
# baseline (speedup 1.0000x reference)
import numpy as np
import concourse.bass as bass
import concourse.mybir as mybir
from concourse.bass_utils import run_bass_kernel_spmd

F32 = mybir.dt.float32
AF = mybir.ActivationFunctionType
ALU = mybir.AluOpType

NCORES = 8
DM = 1024
DI = 2048
DS = 16
DTR = 64
SEQ = 16384
TOK = SEQ // NCORES
DSL = DI // NCORES
NTS = 4
TSZ = TOK // NTS
NCH = 8
EPS = 1e-5


class Sem:
    def __init__(self, name):
        self.name, self.h, self.ctr = name, None, 0


class KB:
    ENGS = ("sync", "scalar", "vector", "tensor", "gpsimd")

    def __init__(self, nc):
        self.nc = nc
        self.streams = {e: [] for e in self.ENGS}
        self.Ls = [Sem(f"L{i}") for i in range(4)]
        self.lrot = 0
        self.S = Sem("S")
        self.V = Sem("V")
        self.P = Sem("P")
        self.G = Sem("G")
        self.GD = Sem("GD")
        self.GA = Sem("GA")
        self.C = Sem("C")
        self.sems = self.Ls + [self.S, self.V, self.P, self.G, self.GD, self.GA,
                               self.C]
        self.esem = {"scalar": self.S, "vector": self.V,
                     "tensor": self.P, "gpsimd": self.G}
        self.bank_free = [None] * 8

    def dma_all(self):
        return [(s, s.ctr) for s in self.Ls if s.ctr]

    def op(self, eng, fn, waits=(), sem=None, incn=1):
        s = sem if sem is not None else self.esem[eng]
        waits = list(waits)
        if eng in ("scalar", "vector", "gpsimd") and s is self.esem.get(eng) and s.ctr:
            waits.append((s, s.ctr))
        s.ctr += incn
        waits = tuple((sm, v) for (sm, v) in waits if isinstance(sm, Sem) and v > 0)
        self.streams[eng].append((fn, waits, s, incn))
        return (s, s.ctr)

    def dma(self, out, in_, waits=(), eng="sync", accum=False, lane=None,
            gsem=None):
        if eng == "gpsimd":
            s = gsem if gsem is not None else self.GD
        else:
            if lane is None:
                s = self.Ls[self.lrot]
                self.lrot = (self.lrot + 1) % len(self.Ls)
            else:
                s = self.Ls[lane]
        if s.ctr:
            waits = list(waits) + [(s, s.ctr)]
        if accum:
            return self.op(eng, lambda e, o=out, i=in_:
                           e.dma_start(out=o, in_=i, accum_op=ALU.add),
                           waits=waits, sem=s, incn=16)
        return self.op(eng, lambda e, o=out, i=in_: e.dma_start(out=o, in_=i),
                       waits=waits, sem=s, incn=16)

    def mm_group(self, bank, mats, ncols=512, waits=(), out_rows=None):
        w = list(waits)
        if self.bank_free[bank] is not None:
            w.append(self.bank_free[bank])
        n = len(mats)
        last = None
        for j, (lhsT, rhs) in enumerate(mats):
            o = self.ps[bank][:, 0:ncols] if out_rows is None else \
                self.ps[bank][out_rows[0]:out_rows[1], 0:ncols]
            last = self.op("tensor",
                           lambda e, oo=o, a=lhsT, r=rhs, st=(j == 0), sp=(j == n - 1):
                           e.matmul(oo, a, r, start=st, stop=sp),
                           waits=(w if j == 0 else ()))
        return last

    def consume_bank(self, bank, eng):
        self.bank_free[bank] = (self.esem[eng], self.esem[eng].ctr)

    def sact(self, out, in_, func, bias=0.0, scale=1.0, waits=()):
        return self.op("scalar", lambda e, o=out, i=in_, f=func, bb=bias, ss=scale:
                       e.activation(o, i, f, bias=bb, scale=ss), waits=waits)

    def scopy(self, out, in_, waits=()):
        return self.op("scalar", lambda e, o=out, i=in_: e.copy(o, i), waits=waits)

    def vtt(self, out, i0, i1, opk, waits=()):
        return self.op("vector", lambda e, o=out, a=i0, c=i1, k=opk:
                       e.tensor_tensor(o, a, c, k), waits=waits)

    def vstt(self, out, i0, scal, i1, op0, op1, waits=()):
        return self.op("vector", lambda e, o=out, a=i0, s=scal, c=i1, k0=op0, k1=op1:
                       e.scalar_tensor_tensor(o, a, s, c, k0, k1), waits=waits)

    def vts(self, out, i0, scal, opk, waits=()):
        return self.op("vector", lambda e, o=out, a=i0, s=scal, k=opk:
                       e.tensor_single_scalar(o, a, s, k), waits=waits)

    def vts2(self, out, i0, s1, s2, op0, op1, waits=()):
        return self.op("vector", lambda e, o=out, a=i0, x=s1, y=s2, k0=op0, k1=op1:
                       e.tensor_scalar(o, a, x, y, k0, k1), waits=waits)

    def vcopy(self, out, in_, waits=()):
        return self.op("vector", lambda e, o=out, i=in_: e.tensor_copy(o, i),
                       waits=waits)

    def gtt(self, out, i0, i1, opk, waits=()):
        return self.op("gpsimd", lambda e, o=out, a=i0, c=i1, k=opk:
                       e.tensor_tensor(o, a, c, k), waits=waits)

    def vscan(self, out, d0, d1, init, waits=()):
        return self.op("vector", lambda e, o=out, a=d0, c=d1, ii=init:
                       e.tensor_tensor_scan(o, a, c, ii, ALU.mult, ALU.add),
                       waits=waits)

    def vmemset(self, ap, val, waits=()):
        return self.op("vector", lambda e, a=ap, v=val: e.memset(a, v), waits=waits)

    def vrecip(self, out, in_, waits=()):
        return self.op("vector", lambda e, o=out, i=in_: e.reciprocal(o, i),
                       waits=waits)

    def coll(self, kind, opk, ins, outs, waits=()):
        return self.op("gpsimd", lambda e, k=kind, o=opk, i=ins, u=outs:
                       e.collective_compute(k, o, replica_groups=[list(range(NCORES))],
                                            ins=i, outs=u),
                       waits=waits, sem=self.C)

    def emit(self, eng_obj, eng_name):
        for fn, waits, sem, incn in self.streams[eng_name]:
            for (sm, val) in waits:
                eng_obj.wait_ge(sm.h, val)
            ins = fn(eng_obj)
            if incn:
                ins.then_inc(sem.h, incn)


def build_program():
    nc = bass.Bass("TRN2", target_bir_lowering=False)

    def inp(name, shape):
        return nc.dram_tensor(name, list(shape), F32, kind="ExternalInput").ap()

    x_fm = inp("x_fm", (DM, TOK))
    winT = {d: inp(f"winT_{d}", (DM, 2 * DSL)) for d in "fb"}
    wxT = {d: inp(f"wxT_{d}", (DSL, 96)) for d in "fb"}
    wdtT = {d: inp(f"wdtT_{d}", (DTR, DSL)) for d in "fb"}
    wconv = {d: inp(f"wconv_{d}", (128, 8)) for d in "fb"}
    bconv = {d: inp(f"bconv_{d}", (128, 2)) for d in "fb"}
    bdt = {d: inp(f"bdt_{d}", (128, 2)) for d in "fb"}
    alog = {d: inp(f"alog_{d}", (128, 32)) for d in "fb"}
    dpv = {d: inp(f"dp_{d}", (128, 2)) for d in "fb"}
    woutc = {d: inp(f"woutc_{d}", (DM, DSL)) for d in "fb"}
    wprojT = {d: inp(f"wprojT_{d}", (DM, DM)) for d in "fb"}
    w_T = inp("w_T", (DM, DM))
    v_rT = inp("v_rT", (32, DM))
    c_T = inp("c_T", (32, DM))
    b_ffn = inp("b_ffn", (128, 8))
    bproj_v = inp("bproj_v", (128, 8))

    out_t = nc.dram_tensor("out", [DM, TOK], F32, kind="ExternalOutput").ap()

    def idram(name, shape):
        return nc.dram_tensor(name, list(shape), F32)

    x1sp = idram("x1sp", (DM, TOK))
    weffsp = idram("weffsp", (DM, DM))
    wtsp = {d: idram(f"wtsp_{d}", (2 * 128, DM)) for d in "fb"}
    xn_in = [idram(f"xn_in{t}", (DM, TSZ)) for t in range(NTS)]
    xn_out = [nc.dram_tensor(f"xn_out{t}", [NCORES * DM, TSZ], F32,
                             addr_space="Shared") for t in range(NTS)]
    dbc_in = {d: [idram(f"dbcin_{d}{c}", (96, TOK)) for c in range(NCH)] for d in "fb"}
    dbc_out = {d: [idram(f"dbcout_{d}{c}", (96, TOK)) for c in range(NCH)]
               for d in "fb"}
    rs_in = [idram(f"rs_in{h}", (NCORES * DM, TOK // 2)) for h in range(2)]
    rs_out = [idram(f"rs_out{h}", (DM, TOK // 2)) for h in range(2)]
    u_sp = idram("u_sp", (2 * 2 * 128, TOK))
    z_sp = idram("z_sp", (2 * 2 * 128, TOK))

    ARENA = 52232
    arena = nc.alloc_sbuf_tensor("arena", [128, ARENA], F32).ap()
    misc = nc.alloc_sbuf_tensor("misc", [128, 800], F32).ap()

    A_SB, WTAP, BCONV, BDT, DPC = 0, 32, 40, 42, 44
    BFFN, HALFB, BPROJ, ONESC, HCAR = 46, 54, 62, 70, 72
    ROWS, ONESR = 104, 640
    CEPS, CINV, CHALF, CNEG1 = 768, 769, 770, 771
    WX0, WX1, WDT, ALOG = ROWS, ROWS + 96, ROWS + 192, ROWS + 448

    def M(o, n=1):
        return misc[:, o:o + n]

    off = [0]

    def alloc(n):
        a = off[0]
        off[0] += n
        assert off[0] <= ARENA, f"arena overflow {off[0]}"
        return a

    R_XT = alloc(4096)
    R_WIN = alloc(4096)
    R_UIN = alloc(4104)
    R_UWSTG = alloc(4096)
    R_ZSTG = alloc(1024)
    R_DBC = alloc(2048)
    R_DL = alloc(4096)
    R_UW = alloc(4096)
    R_BB = alloc(4096)
    R_CB = alloc(4096)
    R_DA = alloc(2048)
    R_DBU = alloc(2048)
    R_H = alloc(2048)
    R_PROD = alloc(2048)
    R_YACC = alloc(4096)
    R_WTS = alloc(2048)
    R_CSTG = alloc(1024)
    R_STG = alloc(1024)
    P1_END = off[0]
    R_X, R_XN, R_TMP, R_SQ = R_XT, R_WIN, R_UIN, R_DL
    R_WEFF = R_UW
    R_WOC = R_CB
    R_WPT = R_YACC
    R_WPT = R_DA
    R_RS = R_UWSTG
    R_CMP = R_YACC
    R_SCR = R_ZSTG
    def A(o, n):
        return arena[:, o:o + n]

    ps = [nc.alloc_psum_tensor(f"ps{i}", [128, 512], F32).ap() for i in range(8)]
    kb = KB(nc)
    kb.ps = ps
    S, V, P, G, GD, C = kb.S, kb.V, kb.P, kb.G, kb.GD, kb.C

    def now(sem):
        return (sem, sem.ctr)

    kb.vmemset(M(ONESC), 1.0)
    kb.vmemset(misc[0:1, ONESR:ONESR + 128], 1.0)
    kb.vmemset(M(CEPS), EPS)
    kb.vmemset(M(CINV), 1.0 / DM)
    kb.vmemset(M(CHALF), 0.5)
    kb.vmemset(M(CNEG1), -1.0)
    l_b1 = kb.dma(M(BFFN, 8), b_ffn)
    l_b2 = kb.dma(M(BPROJ, 8), bproj_v)
    kb.vts(M(HALFB, 8), M(BFFN, 8), 0.5, ALU.mult, waits=[l_b1])

    wT_r = w_T.rearrange("(k p) o -> k p o", p=128)
    l_wT = None
    for k in range(8):
        l_wT = kb.dma(A(R_WEFF + k * 1024, 1024), wT_r[k])
    l_v1 = kb.dma(arena[0:32, R_TMP:R_TMP + DM], v_rT)
    l_v2 = kb.dma(arena[0:32, R_TMP + DM:R_TMP + 2 * DM], c_T)
    w_all = kb.dma_all()
    for k in range(8):
        for hf in range(2):
            bank = hf
            dn = kb.mm_group(bank,
                             [(arena[0:32, R_TMP + k * 128:R_TMP + (k + 1) * 128],
                               arena[0:32, R_TMP + DM + hf * 512:R_TMP + DM + (hf + 1) * 512])],
                             waits=w_all)
            dst = A(R_WEFF + k * 1024 + hf * 512, 512)
            kb.vtt(dst, dst, ps[bank][:, 0:512], ALU.add, waits=[dn])
            kb.consume_bank(bank, "vector")
    v_weff = now(V)
    wsp_r = weffsp.ap().rearrange("(k p) o -> k p o", p=128)
    l_weffsp = None
    for k in range(8):
        l_weffsp = kb.dma(wsp_r[k], A(R_WEFF + k * 1024, 1024), waits=[v_weff])

    for d in "fb":
        wc_r = woutc[d].rearrange("(k p) i -> k p i", p=128)
        wp_r = wprojT[d].rearrange("(k p) o -> k p o", p=128)
        for k in range(8):
            kb.dma(A(R_WOC + k * 256, 256), wc_r[k])
            kb.dma(A(R_WPT + k * 1024, 1024), wp_r[k])
        w_all = kb.dma_all()
        for m in range(2):
            for hf in range(2):
                bank = 2 + hf
                mats = [(A(R_WOC + k * 256 + m * 128, 128),
                         A(R_WPT + k * 1024 + hf * 512, 512)) for k in range(8)]
                dn = kb.mm_group(bank, mats, waits=w_all)
                kb.scopy(A(R_CMP + m * 1024 + hf * 512, 512), ps[bank][:, 0:512],
                         waits=[dn])
                kb.consume_bank(bank, "scalar")
        s_cmp = now(S)
        wt_r = wtsp[d].ap().rearrange("(m p) o -> m p o", p=128)
        for m in range(2):
            kb.dma(wt_r[m], A(R_CMP + m * 1024, 1024), waits=[s_cmp])
    l_wt = kb.dma_all()

    def BO(b, i):
        return b + i * 512 if isinstance(b, int) else b[i]

    def layer_norm(src, dst, ready, dst_guard=None, sq_guard=None):
        rlist = ready if isinstance(ready, list) else None
        sMean = arena[0:1, R_SCR:R_SCR + 512]
        sMsq = arena[0:1, R_SCR + 512:R_SCR + 1024]
        sTmp = misc[0:1, ROWS:ROWS + 512]
        for pb in range(8):
            w = []
            if rlist is not None:
                w.append(rlist[pb])
            elif pb == 0:
                w.append(ready)
            if pb == 0 and sq_guard is not None:
                w.append(sq_guard)
            kb.sact(A(R_SQ + pb * 512, 512), A(src + pb * 512, 512), AF.Square,
                    waits=w)
        s_sq = now(S)
        for pb in range(8):
            w0 = []
            if rlist is not None:
                w0.append(rlist[pb])
            elif pb == 0:
                w0.append(ready)
            if pb == 0 and kb.bank_free[5]:
                w0.append(kb.bank_free[5])
            kb.op("tensor", lambda e, o=ps[5][0:1, 0:512], a=M(ONESC),
                  r=A(src + pb * 512, 512), st=(pb == 0), sp=(pb == 7):
                  e.matmul(o, a, r, start=st, stop=sp), waits=w0)
        dn5 = now(P)
        w6a = [s_sq] + ([kb.bank_free[6]] if kb.bank_free[6] else [])
        for pb in range(8):
            kb.op("tensor", lambda e, o=ps[6][0:1, 0:512], a=M(ONESC),
                  r=A(R_SQ + pb * 512, 512), st=(pb == 0), sp=(pb == 7):
                  e.matmul(o, a, r, start=st, stop=sp),
                  waits=(w6a if pb == 0 else ()))
        dn6 = now(P)
        kb.vts(sMean, ps[5][0:1, 0:512], misc[0:1, CINV:CINV + 1], ALU.mult,
               waits=[dn5])
        kb.consume_bank(5, "vector")
        kb.vts(sMsq, ps[6][0:1, 0:512], misc[0:1, CINV:CINV + 1], ALU.mult,
               waits=[dn6])
        kb.consume_bank(6, "vector")
        kb.vtt(sTmp, sMean, sMean, ALU.mult)
        v_var = kb.vtt(sMsq, sMsq, sTmp, ALU.subtract)
        s_sqr = kb.sact(sTmp, sMsq, AF.Sqrt, bias=misc[0:1, CEPS:CEPS + 1],
                        waits=[v_var])
        v_inv = kb.vrecip(sMsq, sTmp, waits=[s_sqr])
        w6 = [v_inv] + ([kb.bank_free[6]] if kb.bank_free[6] else [])
        dn6b = kb.op("tensor", lambda e: e.matmul(ps[6][:, 0:512],
                     misc[0:1, ONESR:ONESR + 128], sMean,
                     start=True, stop=True), waits=w6)
        w7 = [v_inv] + ([kb.bank_free[7]] if kb.bank_free[7] else [])
        dn7 = kb.op("tensor", lambda e: e.matmul(ps[7][:, 0:512],
                    misc[0:1, ONESR:ONESR + 128], sMsq,
                    start=True, stop=True), waits=w7)
        v_p1 = None
        for pb in range(8):
            v_p1 = kb.vtt(A(R_TMP + pb * 512, 512), A(src + pb * 512, 512),
                          ps[6][:, 0:512], ALU.subtract,
                          waits=([dn6b] if pb == 0 else ()))
        kb.consume_bank(6, "vector")
        ret = None
        for pb in range(8):
            w = [dn7] if pb == 0 else []
            if pb == 0 and dst_guard is not None:
                w.extend(dst_guard if isinstance(dst_guard, list)
                         else [dst_guard])
            ret = kb.vtt(A(BO(dst, pb), 512), A(R_TMP + pb * 512, 512),
                         ps[7][:, 0:512], ALU.mult, waits=w)
        kb.consume_bank(7, "vector")
        return ret, v_p1

    def ffn_apply(xn, src, dst, v_xn, src_ready, dst_guard=None, stat_ready=()):
        evs = []
        dn = None
        for m in range(8):
            bank = m % 4
            mats = [(A(R_WEFF + k * 1024 + m * 128, 128), A(BO(xn, k), 512))
                    for k in range(8)]
            dn = kb.mm_group(bank, mats,
                             waits=[v_xn] + (list(stat_ready) if m == 0 else []))
            s_e = kb.vts2(A(R_TMP + m * 512, 512), ps[bank][:, 0:512], M(CHALF),
                          M(HALFB + m), ALU.mult, ALU.add, waits=[dn])
            w = [s_e]
            if m == 0:
                w.append(src_ready)
                if dst_guard is not None:
                    w.extend(dst_guard if isinstance(dst_guard, list)
                             else [dst_guard])
            ev = kb.vtt(A(dst + m * 512, 512), A(R_TMP + m * 512, 512),
                        A(src + m * 512, 512), ALU.add, waits=w)
            evs.append(ev)
            kb.consume_bank(bank, "vector")
        return evs[-1], evs, dn

    x_r = x_fm.rearrange("(k p) t -> k p t", p=128)
    x1_r = x1sp.ap().rearrange("(k p) t -> k p t", p=128)
    xnin_r = [x.ap().rearrange("(k p) t -> k p t", p=128) for x in xn_in]
    P0X = (R_XT, R_UWSTG)
    P0XN = (R_WIN,
            [R_DBC + i * 512 for i in range(4)] +
            [R_WTS + i * 512 for i in range(4)])
    h0 = {}
    for it in range(NTS + 1):
        ts, pv = it, it - 1
        if ts < NTS:
            p = ts % 2
            cs = slice(ts * TSZ, (ts + 1) * TSZ)
            w = ([h0[ts - 2]["ln2_p1"], h0[ts - 2]["l_x1dma"]]
                 if ts >= 2 else [])
            l_x = kb.dma(A(P0X[p], 4096), (x_r[:, :, cs]).transpose([1, 0, 2]),
                         waits=w)
            h0[ts] = {"l_x": l_x}
        if pv >= 0:
            p = pv % 2
            cs = slice(pv * TSZ, (pv + 1) * TSZ)
            v_x1, evs, pe_ffn = ffn_apply(P0XN[p], P0X[p], P0X[p],
                                          h0[pv]["v_xn1"], h0[pv]["l_x"])
            l_x1dma = kb.dma(x1_r[:, :, cs].transpose([1, 0, 2]),
                             A(P0X[p], 4096), waits=[v_x1])
            h0[pv].update(evs=evs, pe_ffn=pe_ffn, l_x1dma=l_x1dma)
        if ts < NTS:
            p = ts % 2
            dg = ([h0[ts - 2]["pe_ffn"]] + h0[ts - 2]["l_xnd"]
                  if ts >= 2 else None)
            v_xn1, _ = layer_norm(P0X[p], P0XN[p], h0[ts]["l_x"],
                                  dst_guard=dg)
            h0[ts]["v_xn1"] = v_xn1
        if pv >= 0:
            p = pv % 2
            v_xn2, ln2_p1 = layer_norm(P0X[p], P0XN[p], h0[pv]["evs"],
                                       dst_guard=[h0[pv]["pe_ffn"]])
            if p == 0:
                l_xnd = [kb.dma(xnin_r[pv].transpose([1, 0, 2]),
                                A(R_WIN, 4096), waits=[v_xn2])]
            else:
                l_xnd = [kb.dma(xnin_r[pv][0:4].transpose([1, 0, 2]),
                                A(R_DBC, 2048), waits=[v_xn2]),
                         kb.dma(xnin_r[pv][4:8].transpose([1, 0, 2]),
                                A(R_WTS, 2048), waits=[v_xn2])]
            h0[pv].update(ln2_p1=ln2_p1, l_xnd=l_xnd)
            kb.coll("AllGather", ALU.bypass, [xn_in[pv].ap().opt()],
                    [xn_out[pv].ap().opt()], waits=l_xnd)
    c_xn_ts = [(C, C.ctr - (NTS - 1) + t) for t in range(NTS)]

    xn_blk = [x.ap().rearrange("(r k p) t -> r k p t", k=8, p=128)
              for x in xn_out]
    rsin_r = [r.ap().rearrange("(blk k p) t -> blk k p t", k=8, p=128)
              for r in rs_in]
    usp_r = u_sp.ap().rearrange("(s db p) t -> s db p t", db=2, p=128)
    zsp_r = z_sp.ap().rearrange("(s db p) t -> s db p t", db=2, p=128)
    fwd_block_written = {}
    v_yg_prev = None
    p_comp_prev = h0[3]["l_x1dma"]
    c_ar_map = {}
    acc_free = [None, None]
    s_halo = [None]
    stageslot_free = [None, None]
    stg_free = [[]]
    v_conv0 = {}
    dbc_dmas = {}
    p_win_prev = None
    p_delta_prev = None
    u_out_dma = {}
    z_out_dma = {}
    u_load_dma = {}
    z_load_dma = {}
    zstg_free = [h0[3]["ln2_p1"], h0[3]["ln2_p1"]]
    cstg_free = [None, None]
    last_scan_v = {}
    scan_state = {}
    v_w_last = [None]
    last_prod31 = [None]

    for d in ("f", "b"):
        rev = (d == "b")
        gi0 = 0 if d == "f" else NCH
        if d == "f":
            w_win = [h0[2]["pe_ffn"]] + h0[2]["l_xnd"]
            w_wts = list(l_wt) + [h0[3]["pe_ffn"]] + h0[3]["l_xnd"]
            w_rows = [h0[3]["ln2_p1"]]
            w_misc2 = []
            p_win_prev = [h0[2]["ln2_p1"], h0[2]["l_x1dma"]]
            conv0_extra = [h0[3]["pe_ffn"]]
        else:
            w_win = [p_win_prev]
            w_wts = list(l_wt) + [p_comp_prev]
            w_rows = [p_delta_prev] + list(stg_free[0])
            w_misc2 = [acc_free[1], v_w_last[0]]
            conv0_extra = []
        winT_r = winT[d].rearrange("(k p) m -> k p m", p=128)
        for k in range(8):
            kb.dma(A(R_WIN + k * 512, 512), winT_r[k],
                   waits=(w_win if k < 4 else ()))
        wts_r = wtsp[d].ap().rearrange("(m p) o -> m p o", p=128)
        for m in range(2):
            kb.dma(A(R_WTS + m * 1024, 1024), wts_r[m],
                   waits=(w_wts if m == 0 else ()))
        wx_r = wxT[d].rearrange("(c p) m -> c p m", p=128)
        kb.dma(misc[:, WX0:WX0 + 96], wx_r[0], waits=w_rows)
        kb.dma(misc[:, WX1:WX1 + 96], wx_r[1])
        kb.dma(misc[0:DTR, WDT:WDT + 256], wdtT[d])
        kb.dma(M(WTAP, 8), wconv[d], waits=w_misc2)
        kb.dma(M(BCONV, 2), bconv[d])
        kb.dma(M(BDT, 2), bdt[d])
        kb.dma(M(DPC, 2), dpv[d])
        kb.dma(misc[:, ALOG:ALOG + 32], alog[d])
        l_wd = kb.dma_all()
        s_a = kb.sact(M(A_SB, 32), misc[:, ALOG:ALOG + 32], AF.Exp, waits=l_wd)
        kb.vts(M(BDT, 2), M(BDT, 2), -1.0, ALU.mult, waits=l_wd)
        kb.vmemset(M(HCAR, 32), 0.0, waits=[s_a])
        kb.vmemset(arena[:, R_UIN:R_UIN + 4], 0.0)
        v_init = kb.vmemset(arena[:, R_UIN + 2052:R_UIN + 2056], 0.0)

        s_ev_cur = None

        def stageA_ts(c, ts):
            nonlocal p_win_prev, s_ev_cur
            c8 = c % NCH
            if not rev:
                srcx = xn_blk[ts][c8]
                c_xn = c_xn_ts[ts]
            else:
                srcx = xn_blk[NTS - 1 - ts][7 - c8]
                c_xn = c_xn_ts[NTS - 1 - ts]
            w = [c_xn]
            if p_win_prev:
                w.extend(p_win_prev if isinstance(p_win_prev, list)
                         else [p_win_prev])
            l_xt = kb.dma(A(R_XT, 4096), (srcx).transpose([1, 0, 2]), waits=w,
                          lane=2)
            slot = c % 2
            for m in range(4):
                bank = m
                mov = [A(R_XT + k * 512, 512) for k in range(8)]
                if rev:
                    mov = [t[:, ::-1] for t in mov]
                mats = [(A(R_WIN + k * 512 + m * 128, 128), mov[k])
                        for k in range(8)]
                dn = kb.mm_group(bank, mats, waits=[l_xt])
                if m < 2:
                    kb.vcopy(arena[:, R_UIN + m * 2052 + 4 + ts * TSZ:
                                   R_UIN + m * 2052 + 4 + ts * TSZ + 512],
                             ps[bank][:, 0:512], waits=[dn])
                else:
                    db = m - 2
                    zslot = db
                    wv = [dn]
                    if zstg_free[zslot]:
                        wv.append(zstg_free[zslot])
                    s_z = kb.sact(arena[:, R_ZSTG + zslot * 512:R_ZSTG + zslot * 512 + 512],
                                  ps[bank][:, 0:512], AF.Silu, waits=wv)
                    zw = [s_z]
                    if (c - 2) in z_load_dma:
                        zw += z_load_dma[c - 2]
                    dma_t = kb.dma(zsp_r[slot][db][:, ts * TSZ:(ts + 1) * TSZ],
                                   arena[:, R_ZSTG + zslot * 512:R_ZSTG + zslot * 512 + 512],
                                   waits=zw, lane=2)
                    zstg_free[zslot] = dma_t
                    z_out_dma.setdefault(c, []).append(dma_t)
                kb.consume_bank(bank, "scalar" if m >= 2 else "vector")
            p_win_prev = now(P)
            s_ev_cur = now(V)
            slot_a = ts % 2
            for db in range(2):
                base = R_UIN + db * 2052 + 4 + ts * TSZ
                acc = arena[:, R_DBC + slot_a * 1024 + db * 512:
                            R_DBC + slot_a * 1024 + db * 512 + 512]
                wv = []
                if db == 0:
                    if acc_free[slot_a] is not None:
                        wv.append(acc_free[slot_a])
                    if stageslot_free[slot_a] is not None:
                        wv.append(stageslot_free[slot_a])
                    if ts == 0:
                        wv += [v_init]
                        if p_delta_prev:
                            wv.append(p_delta_prev)
                        if s_halo[0] is not None:
                            wv.append(s_halo[0])
                        if c8 == 0:
                            wv += conv0_extra
                kb.vts(acc, arena[:, base - 3:base - 3 + 512],
                       M(WTAP + db * 4), ALU.mult, waits=wv)
                for j in range(1, 4):
                    kb.vstt(acc, arena[:, base - 3 + j:base - 3 + j + 512],
                            M(WTAP + db * 4 + j), acc, ALU.mult, ALU.add)
            v_cv = now(V)
            if ts == 0:
                v_conv0[c] = v_cv
            s_silu = None
            for db in range(2):
                acc = arena[:, R_DBC + slot_a * 1024 + db * 512:
                            R_DBC + slot_a * 1024 + db * 512 + 512]
                sw = [v_cv] if db == 0 else []
                if db == 0:
                    sw += stg_free[0]
                s_silu = kb.sact(A(R_STG + db * 512, 512), acc, AF.Silu,
                                 bias=M(BCONV + db), waits=sw)
            acc_free[slot_a] = s_silu
            spl = []
            for db in range(2):
                uw = [s_silu]
                if (c - 2) in u_load_dma:
                    uw += u_load_dma[c - 2]
                dt_ = kb.dma(usp_r[slot][db][:, ts * TSZ:(ts + 1) * TSZ],
                             A(R_STG + db * 512, 512), waits=uw, lane=2)
                spl.append(dt_)
            u_out_dma.setdefault(c, []).extend(spl)
            mats = [(misc[:, (WX0 if kk == 0 else WX1):
                           (WX0 if kk == 0 else WX1) + 96],
                     A(R_STG + kk * 512, 512)) for kk in range(2)]
            dn4 = kb.mm_group(4, mats, waits=[s_silu] + list(l_wd),
                              out_rows=(0, 96))
            stg_free[0] = [dn4] + spl
            sd = R_DBC + ((ts + 1) % 2) * 1024
            s_dbc = kb.scopy(arena[0:96, sd:sd + 512], ps[4][0:96, 0:512],
                             waits=[dn4])
            kb.consume_bank(4, "scalar")
            l_blk = kb.dma(dbc_in[d][c8].ap()[:, ts * TSZ:(ts + 1) * TSZ],
                           arena[0:96, sd:sd + 512], waits=[s_dbc], lane=2)
            stageslot_free[(ts + 1) % 2] = l_blk
            dbc_dmas.setdefault(c, []).append(l_blk)

        def stageA_tail(c):
            for db in range(2):
                base = R_UIN + db * 2052
                s_halo[0] = kb.scopy(arena[:, base + 1:base + 4],
                                     arena[:, base + 2049:base + 2052],
                                     waits=([v_conv0[c]] if db == 0 else ()))
            c8 = c % NCH
            c_ar_map[c] = kb.coll("AllReduce", ALU.add,
                                  [dbc_in[d][c8].ap().opt()],
                                  [dbc_out[d][c8].ap().opt()],
                                  waits=dbc_dmas[c])

        def bcast_src(c, n, which):
            row = 64 + n if which == "B" else 80 + n
            return dbc_out[d][c % NCH].ap()[row:row + 1, :].unsqueeze(0) \
                .broadcast_to([128, 1, TOK])

        def scan_prologue(c):
            nonlocal p_delta_prev
            st = {"v_scan": {}, "g_row": {}, "v_prod": {}, "l_bc": {}}
            scan_state[c] = st
            c_ar = c_ar_map[c]
            l_dt = kb.dma(arena[0:DTR, R_DBC:R_DBC + 2048],
                          dbc_out[d][c % NCH].ap()[0:DTR, :], waits=[c_ar],
                          lane=3)
            wb = [c_ar]
            if (c - 1) in last_scan_v:
                wb.append(last_scan_v[c - 1])
            b1 = kb.dma(A(R_BB, 2048).rearrange("p (g n) -> p g n", g=1),
                        bcast_src(c, 0, "B"), waits=wb, lane=0)
            b2 = kb.dma(A(R_CB, 2048).rearrange("p (g n) -> p g n", g=1),
                        bcast_src(c, 0, "C"), waits=wb, lane=1)
            st["l_bc"][0] = [b1, b2]
            uw = list(u_out_dma[c])
            if (c - 1) in last_scan_v:
                uw.append(last_scan_v[c - 1])
            lu = [kb.dma(A(R_UW + db * 2048, 2048), usp_r[c % 2][db], waits=uw,
                         lane=3) for db in range(2)]
            u_load_dma[c] = lu
            st_prev = scan_state.get(c - 1)
            st["v_w_db"] = {}
            for m in range(2):
                for ns in range(4):
                    bank = m
                    dn = kb.mm_group(bank,
                                     [(misc[0:DTR, WDT + m * 128:WDT + (m + 1) * 128],
                                       arena[0:DTR, R_DBC + ns * 512:R_DBC + (ns + 1) * 512])],
                                     waits=[l_dt])
                    wv = [dn]
                    if m == 0 and ns == 0 and (c - 1) in last_scan_v:
                        wv.append(last_scan_v[c - 1])
                    kb.sact(A(R_DA + ns * 512, 512), ps[bank][:, 0:512],
                            AF.Sigmoid, bias=M(BDT + m),
                            scale=M(CNEG1), waits=wv)
                    kb.consume_bank(bank, "scalar")
                s_dl = None
                for ns in range(4):
                    s_dl = kb.sact(A(R_DL + m * 2048 + ns * 512, 512),
                                   A(R_DA + ns * 512, 512), AF.Ln)
                db = m
                wv = [lu[db], s_dl]
                if db == 0 and st_prev is not None:
                    wv.append(st_prev["v_yg"])
                st.setdefault("v_init_db", {})[db] = kb.vts(
                    A(R_YACC + db * 2048, 2048), A(R_UW + db * 2048, 2048),
                    M(DPC + db), ALU.mult, waits=wv)
                st["v_w_db"][db] = kb.vstt(A(R_UW + db * 2048, 2048),
                                           A(R_DL + db * 2048, 2048),
                                           -1.0, A(R_UW + db * 2048, 2048),
                                           ALU.mult, ALU.mult)
            p_delta_prev = now(P)
            st["v_w"] = now(V)
            v_w_last[0] = st["v_w"]

        def scan_rows(c, n_list):
            st = scan_state[c]
            v_scan, g_row, v_prod, l_bc = (st["v_scan"], st["g_row"],
                                           st["v_prod"], st["l_bc"])
            for n in n_list:
                buf = n % 2
                if n + 1 < DS and (n + 1) not in l_bc:
                    nbuf = (n + 1) % 2
                    ww = []
                    if n == 0 and last_prod31[0] is not None:
                        ww.append(last_prod31[0])
                    if n >= 1:
                        rp = 2 * n - 1
                        ww = [g_row[rp]]
                        ww.append(v_prod[rp])
                    b1 = kb.dma(A(R_BB + nbuf * 2048, 2048)
                                .rearrange("p (g nn) -> p g nn", g=1),
                                bcast_src(c, n + 1, "B"), waits=ww, lane=0)
                    b2 = kb.dma(A(R_CB + nbuf * 2048, 2048)
                                .rearrange("p (g nn) -> p g nn", g=1),
                                bcast_src(c, n + 1, "C"), waits=ww, lane=1)
                    l_bc[n + 1] = [b1, b2]
                for db in range(2):
                    row = n * 2 + db
                    ch = db * 16 + n
                    wda = [st["v_w"]] if row < 1 else [v_scan[row - 1]]
                    s_da = kb.sact(A(R_DA, 2048), A(R_DL + db * 2048, 2048),
                                   AF.Exp, scale=M(A_SB + ch), waits=wda)
                    kb.vtt(A(R_DBU, 2048), A(R_UW + db * 2048, 2048),
                           A(R_BB + buf * 2048, 2048), ALU.mult,
                           waits=l_bc[n])
                    v_scan[row] = kb.vscan(A(R_H, 2048), A(R_DA, 2048),
                                           A(R_DBU, 2048), M(HCAR + ch),
                                           waits=[s_da])
                    kb.vcopy(M(HCAR + ch), arena[:, R_H + 2047:R_H + 2048])
                    v_prod[row] = kb.vtt(A(R_PROD, 2048), A(R_H, 2048),
                                         A(R_CB + buf * 2048, 2048), ALU.mult,
                                         waits=l_bc[n])
                    g_row[row] = kb.vtt(A(R_YACC + db * 2048, 2048),
                                        A(R_YACC + db * 2048, 2048),
                                        A(R_PROD, 2048), ALU.add)
            if (DS - 1) in n_list:
                last_scan_v[c] = v_scan[2 * DS - 1]
                last_prod31[0] = v_prod[2 * DS - 1]

        def gate_q(c):
            nonlocal v_yg_prev, p_comp_prev
            st = scan_state[c]
            g_row, v_scan = st["g_row"], st["v_scan"]
            slot = c % 2
            v_yg = None
            for db in range(2):
                zw = list(z_out_dma[c]) + [v_scan[2 * DS - 1]]
                if db == 1:
                    zw.append(v_yg)
                lz = kb.dma(A(R_DBU, 2048), zsp_r[slot][db], waits=zw, lane=3)
                z_load_dma.setdefault(c, []).append(lz)
                wv = [lz, g_row[2 * (DS - 1) + db]]
                if db == 0 and p_comp_prev:
                    wv.append(p_comp_prev)
                v_yg = kb.vtt(A(R_UWSTG + db * 2048, 2048),
                              A(R_YACC + db * 2048, 2048),
                              A(R_DBU, 2048), ALU.mult, waits=wv)
            st["v_yg"] = v_yg
            v_yg_prev = v_yg

        def composite_tasks(c):
            st = scan_state[c]
            for ts2 in range(NTS):
                for m in range(8):
                    yield (c, ts2, m, st)

        comp_bank = [0]

        def emit_comp(task):
            nonlocal p_comp_prev
            c, ts2, m, st = task
            c8 = c % NCH
            bank = 5 + comp_bank[0] % 3
            comp_bank[0] += 1
            mats = [(A(R_WTS + kk * 1024 + m * 128, 128),
                     A(R_UWSTG + kk * 2048 + ts2 * TSZ, 512))
                    for kk in range(2)]
            dn = kb.mm_group(bank, mats, waits=[st["v_yg"]])
            cslot = (ts2 * 8 + m) % 2
            wv = [dn]
            if cstg_free[cslot]:
                wv.append(cstg_free[cslot])
            dst = arena[:, R_CSTG + cslot * 512:R_CSTG + cslot * 512 + 512]
            if rev:
                dst = dst[:, ::-1]
            s_pe = kb.scopy(dst, ps[bank][:, 0:512], waits=wv)
            kb.consume_bank(bank, "scalar")
            if not rev:
                col0 = ts2 * TSZ
                hx, hc = col0 // (TOK // 2), col0 % (TOK // 2)
                dma_t = kb.dma(
                    rsin_r[hx][c8][m][:, hc:hc + TSZ],
                    arena[:, R_CSTG + cslot * 512:R_CSTG + cslot * 512 + 512],
                    waits=[s_pe], lane=3)
            else:
                blk = 7 - c8
                col0 = TOK - (ts2 + 1) * TSZ
                hx, hc = col0 // (TOK // 2), col0 % (TOK // 2)
                dma_t = kb.dma(
                    rsin_r[hx][blk][m][:, hc:hc + TSZ],
                    arena[:, R_CSTG + cslot * 512:R_CSTG + cslot * 512 + 512],
                    waits=[s_pe] + fwd_block_written[blk],
                    eng="gpsimd", accum=True)
            cstg_free[cslot] = dma_t
            if ts2 == NTS - 1 and m == 7:
                if not rev:
                    fwd_block_written[c8] = kb.dma_all()
                p_comp_prev = now(P)
                del scan_state[c]

        ROW_SLOTS = {0: range(0, 2), 1: range(2, 5), 2: range(5, 8),
                     3: range(8, 11)}
        pending_comp = []

        def drain_comp(k):
            for _ in range(k):
                if pending_comp:
                    emit_comp(pending_comp.pop(0))

        def scan_rows_drip(c, n_list):
            for n in n_list:
                scan_rows(c, [n])
                drain_comp(2)

        for i in range(NCH + 1):
            if i >= 1:
                scan_prologue(gi0 + i - 1)
                scan_rows_drip(gi0 + i - 1, ROW_SLOTS[0])
            for ts in range(NTS):
                if i < NCH:
                    stageA_ts(gi0 + i, ts)
                if i >= 1 and ts >= 1:
                    scan_rows_drip(gi0 + i - 1, ROW_SLOTS[ts])
            if i < NCH:
                stageA_tail(gi0 + i)
            if i >= 1:
                scan_rows_drip(gi0 + i - 1, range(11, DS))
                drain_comp(99)
                gate_q(gi0 + i - 1)
                pending_comp = list(composite_tasks(gi0 + i - 1))
        drain_comp(99)
    rs_bar = kb.dma_all() + [(GD, GD.ctr), (kb.GA, kb.GA.ctr)]
    c_rs = [kb.coll("ReduceScatter", ALU.add, [rs_in[h].ap().opt()],
                    [rs_out[h].ap().opt()], waits=rs_bar) for h in range(2)]

    p3_bar = [now(S), now(V), now(P), now(G), (GD, GD.ctr)] + kb.dma_all()
    for k in range(8):
        kb.dma(A(R_WEFF + k * 1024, 1024), wsp_r[k],
               waits=(p3_bar if k < 4 else ()))
    l_weff2 = kb.dma_all()
    rs_r = [r.ap().rearrange("(k p) t -> k p t", p=128) for r in rs_out]
    out_r = out_t.rearrange("(k p) t -> k p t", p=128)
    P3X = (R_XT, R_CB)
    P3XN = (R_WIN, R_DA)
    P3RS = (R_UWSTG, R_H)
    h3 = {}
    for it in range(NTS + 1):
        ts, pv = it, it - 1
        if ts < NTS:
            p = ts % 2
            cs = slice(ts * TSZ, (ts + 1) * TSZ)
            w = ([h3[ts - 2]["ln_p1"], h3[ts - 2]["l_outdma"]]
                 if ts >= 2 else [])
            l_x1 = kb.dma(A(P3X[p], 4096),
                          (x1_r[:, :, cs]).transpose([1, 0, 2]), waits=w)
            hx, hc = (ts * TSZ) // (TOK // 2), (ts * TSZ) % (TOK // 2)
            wt = [c_rs[hx]] + ([h3[ts - 2]["v_t"]] if ts >= 2 else [])
            l_t = kb.dma(A(P3RS[p], 4096),
                         (rs_r[hx][:, :, hc:hc + TSZ]).transpose([1, 0, 2]),
                         waits=wt)
            for pb in range(8):
                kb.vts(A(R_TMP + pb * 512, 512), A(P3RS[p] + pb * 512, 512),
                       M(BPROJ + pb), ALU.add, waits=([l_t] if pb == 0 else ()))
            v_t = now(V)
            evs_x2 = []
            for pb in range(8):
                ev = kb.vtt(A(P3X[p] + pb * 512, 512),
                            A(R_TMP + pb * 512, 512),
                            A(P3X[p] + pb * 512, 512), ALU.add,
                            waits=([l_x1] if pb == 0 else ()))
                evs_x2.append(ev)
            h3[ts] = {"l_x1": l_x1, "v_t": v_t, "evs_x2": evs_x2}
        if pv >= 0:
            p = pv % 2
            cs = slice(pv * TSZ, (pv + 1) * TSZ)
            v_x3, _, pe_ffn = ffn_apply(P3XN[p], P3X[p], P3X[p],
                                        h3[pv]["v_xn3"], h3[pv]["evs_x2"][-1],
                                        stat_ready=l_weff2)
            l_outdma = kb.dma(out_r[:, :, cs].transpose([1, 0, 2]),
                              A(P3X[p], 4096), waits=[v_x3])
            h3[pv].update(pe_ffn=pe_ffn, l_outdma=l_outdma)
        if ts < NTS:
            p = ts % 2
            dg = [h3[ts - 2]["pe_ffn"]] if ts >= 2 else None
            v_xn3, ln_p1 = layer_norm(P3X[p], P3XN[p], h3[ts]["evs_x2"],
                                      dst_guard=dg)
            h3[ts].update(v_xn3=v_xn3, ln_p1=ln_p1)
    final_waits = kb.dma_all()
    kb.op("sync", lambda e: [e.wait_ge(sm.h, v) for sm, v in final_waits],
          sem=kb.Ls[0], incn=0)

    return nc, kb


def _emit(nc, kb):
    import contextlib
    with contextlib.ExitStack() as st:
        for s in kb.sems:
            s.h = st.enter_context(nc.semaphore(s.name))
        blk = st.enter_context(nc.Block())
        blk.sync(lambda e: kb.emit(e, "sync"))
        blk.scalar(lambda e: kb.emit(e, "scalar"))
        blk.vector(lambda e: kb.emit(e, "vector"))
        blk.tensor(lambda e: kb.emit(e, "tensor"))
        blk.gpsimd(lambda e: kb.emit(e, "gpsimd"))
    return nc


def _host_inputs(c, a):
    t0, t1 = c * TOK, (c + 1) * TOK
    d0, d1 = c * DSL, (c + 1) * DSL
    ct = np.ascontiguousarray
    m = {}
    m["x_fm"] = ct(a["L0_attn_x"][0, t0:t1, :].T)
    for d, sfx in (("f", "_f"), ("b", "_b")):
        winT = a["L0_attn_win" + sfx].T
        m[f"winT_{d}"] = ct(np.concatenate(
            [winT[:, d0:d1], winT[:, DI + d0:DI + d1]], axis=1))
        m[f"wxT_{d}"] = ct(a["L0_attn_wx" + sfx].T[d0:d1, :])
        m[f"wdtT_{d}"] = ct(a["L0_attn_wdt" + sfx].T[:, d0:d1])
        m[f"wconv_{d}"] = ct(a["L0_attn_wconv" + sfx][d0:d1, 0, :]
                             .reshape(2, 128, 4).transpose(1, 0, 2).reshape(128, 8))
        m[f"bconv_{d}"] = ct(a["L0_attn_bconv" + sfx][d0:d1].reshape(2, 128).T)
        m[f"bdt_{d}"] = ct(a["L0_attn_bdt" + sfx][d0:d1].reshape(2, 128).T)
        m[f"alog_{d}"] = ct(a["L0_attn_Alog" + sfx][d0:d1]
                            .reshape(2, 128, DS).transpose(1, 0, 2).reshape(128, 32))
        m[f"dp_{d}"] = ct(a["L0_attn_Dp" + sfx][d0:d1].reshape(2, 128).T)
        m[f"woutc_{d}"] = ct(a["L0_attn_wout" + sfx][:, d0:d1])
        half = a["L0_attn_wproj"][:, :DM] if d == "f" else a["L0_attn_wproj"][:, DM:]
        m[f"wprojT_{d}"] = ct(half.T)
    m["w_T"] = ct(a["L0_ffn_W"].T)
    m["v_rT"] = ct(a["L0_ffn_V_r"].T)
    m["c_T"] = ct(a["L0_ffn_C"].T)
    m["b_ffn"] = ct(a["L0_ffn_b"].reshape(8, 128).T)
    m["bproj_v"] = ct(a["L0_attn_bproj"].reshape(8, 128).T)
    return m


def kernel(**inputs):
    a = {k: np.asarray(v, dtype=np.float32) for k, v in inputs.items()}
    nc, kb = build_program()
    _emit(nc, kb)
    in_maps = [_host_inputs(c, a) for c in range(NCORES)]
    res = run_bass_kernel_spmd(nc, in_maps, core_ids=list(range(NCORES)))
    outs = [np.asarray(res.results[c]["out"]).T for c in range(NCORES)]
    return np.concatenate(outs, axis=0).reshape(1, SEQ, DM)


# version 19
# speedup vs baseline: 1.0123x; 1.0123x over previous
import numpy as np
import concourse.bass as bass
import concourse.mybir as mybir
from concourse.bass_utils import run_bass_kernel_spmd

F32 = mybir.dt.float32
AF = mybir.ActivationFunctionType
ALU = mybir.AluOpType

NCORES = 8
DM = 1024
DI = 2048
DS = 16
DTR = 64
SEQ = 16384
TOK = SEQ // NCORES
DSL = DI // NCORES
NTS = 4
TSZ = TOK // NTS
NCH = 8
EPS = 1e-5


class Sem:
    def __init__(self, name):
        self.name, self.h, self.ctr = name, None, 0


class KB:
    ENGS = ("sync", "scalar", "vector", "tensor", "gpsimd")

    def __init__(self, nc):
        self.nc = nc
        self.streams = {e: [] for e in self.ENGS}
        self.Ls = [Sem(f"L{i}") for i in range(4)]
        self.lrot = 0
        self.S = Sem("S")
        self.V = Sem("V")
        self.P = Sem("P")
        self.G = Sem("G")
        self.GD = Sem("GD")
        self.GA = Sem("GA")
        self.C = Sem("C")
        self.sems = self.Ls + [self.S, self.V, self.P, self.G, self.GD, self.GA,
                               self.C]
        self.esem = {"scalar": self.S, "vector": self.V,
                     "tensor": self.P, "gpsimd": self.G}
        self.bank_free = [None] * 8

    def dma_all(self):
        return [(s, s.ctr) for s in self.Ls if s.ctr]

    def op(self, eng, fn, waits=(), sem=None, incn=1):
        s = sem if sem is not None else self.esem[eng]
        waits = list(waits)
        if eng in ("scalar", "vector", "gpsimd") and s is self.esem.get(eng) and s.ctr:
            waits.append((s, s.ctr))
        s.ctr += incn
        waits = tuple((sm, v) for (sm, v) in waits if isinstance(sm, Sem) and v > 0)
        self.streams[eng].append((fn, waits, s, incn))
        return (s, s.ctr)

    def dma(self, out, in_, waits=(), eng="sync", accum=False, lane=None,
            gsem=None):
        if eng == "gpsimd":
            s = gsem if gsem is not None else self.GD
        else:
            if lane is None:
                s = self.Ls[self.lrot]
                self.lrot = (self.lrot + 1) % len(self.Ls)
            else:
                s = self.Ls[lane]
        if s.ctr:
            waits = list(waits) + [(s, s.ctr)]
        if accum:
            return self.op(eng, lambda e, o=out, i=in_:
                           e.dma_start(out=o, in_=i, accum_op=ALU.add),
                           waits=waits, sem=s, incn=16)
        return self.op(eng, lambda e, o=out, i=in_: e.dma_start(out=o, in_=i),
                       waits=waits, sem=s, incn=16)

    def mm_group(self, bank, mats, ncols=512, waits=(), out_rows=None):
        w = list(waits)
        if self.bank_free[bank] is not None:
            w.append(self.bank_free[bank])
        n = len(mats)
        last = None
        for j, (lhsT, rhs) in enumerate(mats):
            o = self.ps[bank][:, 0:ncols] if out_rows is None else \
                self.ps[bank][out_rows[0]:out_rows[1], 0:ncols]
            last = self.op("tensor",
                           lambda e, oo=o, a=lhsT, r=rhs, st=(j == 0), sp=(j == n - 1):
                           e.matmul(oo, a, r, start=st, stop=sp),
                           waits=(w if j == 0 else ()))
        return last

    def consume_bank(self, bank, eng):
        self.bank_free[bank] = (self.esem[eng], self.esem[eng].ctr)

    def sact(self, out, in_, func, bias=0.0, scale=1.0, waits=()):
        return self.op("scalar", lambda e, o=out, i=in_, f=func, bb=bias, ss=scale:
                       e.activation(o, i, f, bias=bb, scale=ss), waits=waits)

    def scopy(self, out, in_, waits=()):
        return self.op("scalar", lambda e, o=out, i=in_: e.copy(o, i), waits=waits)

    def vtt(self, out, i0, i1, opk, waits=()):
        return self.op("vector", lambda e, o=out, a=i0, c=i1, k=opk:
                       e.tensor_tensor(o, a, c, k), waits=waits)

    def vstt(self, out, i0, scal, i1, op0, op1, waits=()):
        return self.op("vector", lambda e, o=out, a=i0, s=scal, c=i1, k0=op0, k1=op1:
                       e.scalar_tensor_tensor(o, a, s, c, k0, k1), waits=waits)

    def vts(self, out, i0, scal, opk, waits=()):
        return self.op("vector", lambda e, o=out, a=i0, s=scal, k=opk:
                       e.tensor_single_scalar(o, a, s, k), waits=waits)

    def vts2(self, out, i0, s1, s2, op0, op1, waits=()):
        return self.op("vector", lambda e, o=out, a=i0, x=s1, y=s2, k0=op0, k1=op1:
                       e.tensor_scalar(o, a, x, y, k0, k1), waits=waits)

    def vcopy(self, out, in_, waits=()):
        return self.op("vector", lambda e, o=out, i=in_: e.tensor_copy(o, i),
                       waits=waits)

    def gtt(self, out, i0, i1, opk, waits=()):
        return self.op("gpsimd", lambda e, o=out, a=i0, c=i1, k=opk:
                       e.tensor_tensor(o, a, c, k), waits=waits)

    def vscan(self, out, d0, d1, init, waits=()):
        return self.op("vector", lambda e, o=out, a=d0, c=d1, ii=init:
                       e.tensor_tensor_scan(o, a, c, ii, ALU.mult, ALU.add),
                       waits=waits)

    def vmemset(self, ap, val, waits=()):
        return self.op("vector", lambda e, a=ap, v=val: e.memset(a, v), waits=waits)

    def vrecip(self, out, in_, waits=()):
        return self.op("vector", lambda e, o=out, i=in_: e.reciprocal(o, i),
                       waits=waits)

    def coll(self, kind, opk, ins, outs, waits=()):
        return self.op("gpsimd", lambda e, k=kind, o=opk, i=ins, u=outs:
                       e.collective_compute(k, o, replica_groups=[list(range(NCORES))],
                                            ins=i, outs=u),
                       waits=waits, sem=self.C)

    def emit(self, eng_obj, eng_name):
        for fn, waits, sem, incn in self.streams[eng_name]:
            for (sm, val) in waits:
                eng_obj.wait_ge(sm.h, val)
            ins = fn(eng_obj)
            if incn:
                ins.then_inc(sem.h, incn)


def build_program():
    nc = bass.Bass("TRN2", target_bir_lowering=False)

    def inp(name, shape):
        return nc.dram_tensor(name, list(shape), F32, kind="ExternalInput").ap()

    x_fm = inp("x_fm", (DM, TOK))
    winT = {d: inp(f"winT_{d}", (DM, 2 * DSL)) for d in "fb"}
    wxT = {d: inp(f"wxT_{d}", (DSL, 96)) for d in "fb"}
    wdtT = {d: inp(f"wdtT_{d}", (DTR, DSL)) for d in "fb"}
    wconv = {d: inp(f"wconv_{d}", (128, 8)) for d in "fb"}
    bconv = {d: inp(f"bconv_{d}", (128, 2)) for d in "fb"}
    bdt = {d: inp(f"bdt_{d}", (128, 2)) for d in "fb"}
    alog = {d: inp(f"alog_{d}", (128, 32)) for d in "fb"}
    dpv = {d: inp(f"dp_{d}", (128, 2)) for d in "fb"}
    woutc = {d: inp(f"woutc_{d}", (DM, DSL)) for d in "fb"}
    wprojT = {d: inp(f"wprojT_{d}", (DM, DM)) for d in "fb"}
    w_T = inp("w_T", (DM, DM))
    v_rT = inp("v_rT", (32, DM))
    c_T = inp("c_T", (32, DM))
    b_ffn = inp("b_ffn", (128, 8))
    bproj_v = inp("bproj_v", (128, 8))

    out_t = nc.dram_tensor("out", [DM, TOK], F32, kind="ExternalOutput").ap()

    def idram(name, shape):
        return nc.dram_tensor(name, list(shape), F32)

    x1sp = idram("x1sp", (DM, TOK))
    weffsp = idram("weffsp", (DM, DM))
    wtsp = {d: idram(f"wtsp_{d}", (2 * 128, DM)) for d in "fb"}
    xn_in = [idram(f"xn_in{t}", (DM, TSZ)) for t in range(NTS)]
    xn_out = [nc.dram_tensor(f"xn_out{t}", [NCORES * DM, TSZ], F32,
                             addr_space="Shared") for t in range(NTS)]
    dbc_in = {d: [idram(f"dbcin_{d}{c}", (96, TOK)) for c in range(NCH)] for d in "fb"}
    dbc_out = {d: [idram(f"dbcout_{d}{c}", (96, TOK)) for c in range(NCH)]
               for d in "fb"}
    rs_in = [idram(f"rs_in{h}", (NCORES * DM, TSZ)) for h in range(NTS)]
    rs_out = [idram(f"rs_out{h}", (DM, TSZ)) for h in range(NTS)]
    u_sp = idram("u_sp", (2 * 2 * 128, TOK))
    z_sp = idram("z_sp", (2 * 2 * 128, TOK))

    ARENA = 52232
    arena = nc.alloc_sbuf_tensor("arena", [128, ARENA], F32).ap()
    misc = nc.alloc_sbuf_tensor("misc", [128, 800], F32).ap()

    A_SB, WTAP, BCONV, BDT, DPC = 0, 32, 40, 42, 44
    BFFN, HALFB, BPROJ, ONESC, HCAR = 46, 54, 62, 70, 72
    ROWS, ONESR = 104, 640
    CEPS, CINV, CHALF, CNEG1 = 768, 769, 770, 771
    WX0, WX1, WDT, ALOG = ROWS, ROWS + 96, ROWS + 192, ROWS + 448

    def M(o, n=1):
        return misc[:, o:o + n]

    off = [0]

    def alloc(n):
        a = off[0]
        off[0] += n
        assert off[0] <= ARENA, f"arena overflow {off[0]}"
        return a

    R_XT = alloc(4096)
    R_WIN = alloc(4096)
    R_UIN = alloc(4104)
    R_UWSTG = alloc(4096)
    R_ZSTG = alloc(1024)
    R_DBC = alloc(2048)
    R_DL = alloc(4096)
    R_UW = alloc(4096)
    R_BB = alloc(4096)
    R_CB = alloc(4096)
    R_DA = alloc(2048)
    R_DBU = alloc(2048)
    R_H = alloc(2048)
    R_PROD = alloc(2048)
    R_YACC = alloc(4096)
    R_WTS = alloc(2048)
    R_CSTG = alloc(1024)
    R_STG = alloc(1024)
    P1_END = off[0]
    R_X, R_XN, R_TMP, R_SQ = R_XT, R_WIN, R_UIN, R_DL
    R_WEFF = R_UW
    R_WOC = R_CB
    R_WPT = R_YACC
    R_WPT = R_DA
    R_RS = R_UWSTG
    R_CMP = R_YACC
    R_SCR = R_ZSTG
    def A(o, n):
        return arena[:, o:o + n]

    ps = [nc.alloc_psum_tensor(f"ps{i}", [128, 512], F32).ap() for i in range(8)]
    kb = KB(nc)
    kb.ps = ps
    S, V, P, G, GD, C = kb.S, kb.V, kb.P, kb.G, kb.GD, kb.C

    def now(sem):
        return (sem, sem.ctr)

    kb.vmemset(M(ONESC), 1.0)
    kb.vmemset(misc[0:1, ONESR:ONESR + 128], 1.0)
    kb.vmemset(M(CEPS), EPS)
    kb.vmemset(M(CINV), 1.0 / DM)
    kb.vmemset(M(CHALF), 0.5)
    kb.vmemset(M(CNEG1), -1.0)
    l_b1 = kb.dma(M(BFFN, 8), b_ffn)
    l_b2 = kb.dma(M(BPROJ, 8), bproj_v)
    kb.vts(M(HALFB, 8), M(BFFN, 8), 0.5, ALU.mult, waits=[l_b1])

    x_r0 = x_fm.rearrange("(k p) t -> k p t", p=128)
    pre_lx = {}
    for ts0 in range(2):
        dst0 = (R_XT, R_UWSTG)[ts0]
        pre_lx[ts0] = kb.dma(A(dst0, 4096),
                             (x_r0[:, :, ts0 * TSZ:(ts0 + 1) * TSZ])
                             .transpose([1, 0, 2]))
    wT_r = w_T.rearrange("(k p) o -> k p o", p=128)
    l_wT = None
    for k in range(8):
        l_wT = kb.dma(A(R_WEFF + k * 1024, 1024), wT_r[k])
    l_v1 = kb.dma(arena[0:32, R_TMP:R_TMP + DM], v_rT)
    l_v2 = kb.dma(arena[0:32, R_TMP + DM:R_TMP + 2 * DM], c_T)
    w_all = kb.dma_all()
    for k in range(8):
        for hf in range(2):
            bank = hf
            dn = kb.mm_group(bank,
                             [(arena[0:32, R_TMP + k * 128:R_TMP + (k + 1) * 128],
                               arena[0:32, R_TMP + DM + hf * 512:R_TMP + DM + (hf + 1) * 512])],
                             waits=w_all)
            dst = A(R_WEFF + k * 1024 + hf * 512, 512)
            kb.vtt(dst, dst, ps[bank][:, 0:512], ALU.add, waits=[dn])
            kb.consume_bank(bank, "vector")
    v_weff = now(V)
    wsp_r = weffsp.ap().rearrange("(k p) o -> k p o", p=128)
    l_weffsp = None
    for k in range(8):
        l_weffsp = kb.dma(wsp_r[k], A(R_WEFF + k * 1024, 1024), waits=[v_weff])

    for d in "fb":
        wc_r = woutc[d].rearrange("(k p) i -> k p i", p=128)
        wp_r = wprojT[d].rearrange("(k p) o -> k p o", p=128)
        for k in range(8):
            kb.dma(A(R_WOC + k * 256, 256), wc_r[k])
            kb.dma(A(R_WPT + k * 1024, 1024), wp_r[k])
        w_all = kb.dma_all()
        for m in range(2):
            for hf in range(2):
                bank = 2 + hf
                mats = [(A(R_WOC + k * 256 + m * 128, 128),
                         A(R_WPT + k * 1024 + hf * 512, 512)) for k in range(8)]
                dn = kb.mm_group(bank, mats, waits=w_all)
                kb.scopy(A(R_CMP + m * 1024 + hf * 512, 512), ps[bank][:, 0:512],
                         waits=[dn])
                kb.consume_bank(bank, "scalar")
        s_cmp = now(S)
        wt_r = wtsp[d].ap().rearrange("(m p) o -> m p o", p=128)
        for m in range(2):
            kb.dma(wt_r[m], A(R_CMP + m * 1024, 1024), waits=[s_cmp])
    l_wt = kb.dma_all()

    def BO(b, i):
        return b + i * 512 if isinstance(b, int) else b[i]

    def layer_norm(src, dst, ready, dst_guard=None, sq_guard=None):
        rlist = ready if isinstance(ready, list) else None
        sMean = arena[0:1, R_SCR:R_SCR + 512]
        sMsq = arena[0:1, R_SCR + 512:R_SCR + 1024]
        sTmp = misc[0:1, ROWS:ROWS + 512]
        for pb in range(8):
            w = []
            if rlist is not None:
                w.append(rlist[pb])
            elif pb == 0:
                w.append(ready)
            if pb == 0 and sq_guard is not None:
                w.append(sq_guard)
            kb.sact(A(R_SQ + pb * 512, 512), A(src + pb * 512, 512), AF.Square,
                    waits=w)
        s_sq = now(S)
        for pb in range(8):
            w0 = []
            if rlist is not None:
                w0.append(rlist[pb])
            elif pb == 0:
                w0.append(ready)
            if pb == 0 and kb.bank_free[5]:
                w0.append(kb.bank_free[5])
            kb.op("tensor", lambda e, o=ps[5][0:1, 0:512], a=M(ONESC),
                  r=A(src + pb * 512, 512), st=(pb == 0), sp=(pb == 7):
                  e.matmul(o, a, r, start=st, stop=sp), waits=w0)
        dn5 = now(P)
        w6a = [s_sq] + ([kb.bank_free[6]] if kb.bank_free[6] else [])
        for pb in range(8):
            kb.op("tensor", lambda e, o=ps[6][0:1, 0:512], a=M(ONESC),
                  r=A(R_SQ + pb * 512, 512), st=(pb == 0), sp=(pb == 7):
                  e.matmul(o, a, r, start=st, stop=sp),
                  waits=(w6a if pb == 0 else ()))
        dn6 = now(P)
        kb.vts(sMean, ps[5][0:1, 0:512], misc[0:1, CINV:CINV + 1], ALU.mult,
               waits=[dn5])
        kb.consume_bank(5, "vector")
        kb.vts(sMsq, ps[6][0:1, 0:512], misc[0:1, CINV:CINV + 1], ALU.mult,
               waits=[dn6])
        kb.consume_bank(6, "vector")
        kb.vtt(sTmp, sMean, sMean, ALU.mult)
        v_var = kb.vtt(sMsq, sMsq, sTmp, ALU.subtract)
        s_sqr = kb.sact(sTmp, sMsq, AF.Sqrt, bias=misc[0:1, CEPS:CEPS + 1],
                        waits=[v_var])
        v_inv = kb.vrecip(sMsq, sTmp, waits=[s_sqr])
        w6 = [v_inv] + ([kb.bank_free[6]] if kb.bank_free[6] else [])
        dn6b = kb.op("tensor", lambda e: e.matmul(ps[6][:, 0:512],
                     misc[0:1, ONESR:ONESR + 128], sMean,
                     start=True, stop=True), waits=w6)
        w7 = [v_inv] + ([kb.bank_free[7]] if kb.bank_free[7] else [])
        dn7 = kb.op("tensor", lambda e: e.matmul(ps[7][:, 0:512],
                    misc[0:1, ONESR:ONESR + 128], sMsq,
                    start=True, stop=True), waits=w7)
        v_p1 = None
        for pb in range(8):
            v_p1 = kb.vtt(A(R_TMP + pb * 512, 512), A(src + pb * 512, 512),
                          ps[6][:, 0:512], ALU.subtract,
                          waits=([dn6b] if pb == 0 else ()))
        kb.consume_bank(6, "vector")
        ret = None
        for pb in range(8):
            w = [dn7] if pb == 0 else []
            if pb == 0 and dst_guard is not None:
                w.extend(dst_guard if isinstance(dst_guard, list)
                         else [dst_guard])
            ret = kb.vtt(A(BO(dst, pb), 512), A(R_TMP + pb * 512, 512),
                         ps[7][:, 0:512], ALU.mult, waits=w)
        kb.consume_bank(7, "vector")
        return ret, v_p1

    def ffn_apply(xn, src, dst, v_xn, src_ready, dst_guard=None, stat_ready=()):
        evs = []
        dn = None
        for m in range(8):
            bank = m % 4
            mats = [(A(R_WEFF + k * 1024 + m * 128, 128), A(BO(xn, k), 512))
                    for k in range(8)]
            dn = kb.mm_group(bank, mats,
                             waits=[v_xn] + (list(stat_ready) if m == 0 else []))
            s_e = kb.vts2(A(R_TMP + m * 512, 512), ps[bank][:, 0:512], M(CHALF),
                          M(HALFB + m), ALU.mult, ALU.add, waits=[dn])
            w = [s_e]
            if m == 0:
                w.append(src_ready)
                if dst_guard is not None:
                    w.extend(dst_guard if isinstance(dst_guard, list)
                             else [dst_guard])
            ev = kb.vtt(A(dst + m * 512, 512), A(R_TMP + m * 512, 512),
                        A(src + m * 512, 512), ALU.add, waits=w)
            evs.append(ev)
            kb.consume_bank(bank, "vector")
        return evs[-1], evs, dn

    x_r = x_fm.rearrange("(k p) t -> k p t", p=128)
    x1_r = x1sp.ap().rearrange("(k p) t -> k p t", p=128)
    xnin_r = [x.ap().rearrange("(k p) t -> k p t", p=128) for x in xn_in]
    P0X = (R_XT, R_UWSTG)
    P0XN = (R_WIN,
            [R_DBC + i * 512 for i in range(4)] +
            [R_WTS + i * 512 for i in range(4)])
    h0 = {}
    for it in range(NTS + 1):
        ts, pv = it, it - 1
        if ts < NTS:
            p = ts % 2
            cs = slice(ts * TSZ, (ts + 1) * TSZ)
            w = ([h0[ts - 2]["ln2_p1"], h0[ts - 2]["l_x1dma"]]
                 if ts >= 2 else [])
            l_x = pre_lx.get(ts)
            if l_x is None:
                l_x = kb.dma(A(P0X[p], 4096),
                             (x_r[:, :, cs]).transpose([1, 0, 2]), waits=w)
            h0[ts] = {"l_x": l_x}
        if pv >= 0:
            p = pv % 2
            cs = slice(pv * TSZ, (pv + 1) * TSZ)
            v_x1, evs, pe_ffn = ffn_apply(P0XN[p], P0X[p], P0X[p],
                                          h0[pv]["v_xn1"], h0[pv]["l_x"])
            l_x1dma = kb.dma(x1_r[:, :, cs].transpose([1, 0, 2]),
                             A(P0X[p], 4096), waits=[v_x1])
            h0[pv].update(evs=evs, pe_ffn=pe_ffn, l_x1dma=l_x1dma)
        if ts < NTS:
            p = ts % 2
            dg = ([h0[ts - 2]["pe_ffn"]] + h0[ts - 2]["l_xnd"]
                  if ts >= 2 else None)
            v_xn1, _ = layer_norm(P0X[p], P0XN[p], h0[ts]["l_x"],
                                  dst_guard=dg)
            h0[ts]["v_xn1"] = v_xn1
        if pv >= 0:
            p = pv % 2
            v_xn2, ln2_p1 = layer_norm(P0X[p], P0XN[p], h0[pv]["evs"],
                                       dst_guard=[h0[pv]["pe_ffn"]])
            if p == 0:
                l_xnd = [kb.dma(xnin_r[pv].transpose([1, 0, 2]),
                                A(R_WIN, 4096), waits=[v_xn2])]
            else:
                l_xnd = [kb.dma(xnin_r[pv][0:4].transpose([1, 0, 2]),
                                A(R_DBC, 2048), waits=[v_xn2]),
                         kb.dma(xnin_r[pv][4:8].transpose([1, 0, 2]),
                                A(R_WTS, 2048), waits=[v_xn2])]
            h0[pv].update(ln2_p1=ln2_p1, l_xnd=l_xnd)
            kb.coll("AllGather", ALU.bypass, [xn_in[pv].ap().opt()],
                    [xn_out[pv].ap().opt()], waits=l_xnd)
    c_xn_ts = [(C, C.ctr - (NTS - 1) + t) for t in range(NTS)]

    xn_blk = [x.ap().rearrange("(r k p) t -> r k p t", k=8, p=128)
              for x in xn_out]
    rsin_r = [r.ap().rearrange("(blk k p) t -> blk k p t", k=8, p=128)
              for r in rs_in]
    c_rs4 = {}
    usp_r = u_sp.ap().rearrange("(s db p) t -> s db p t", db=2, p=128)
    zsp_r = z_sp.ap().rearrange("(s db p) t -> s db p t", db=2, p=128)
    fwd_block_written = {}
    v_yg_prev = None
    p_comp_prev = h0[3]["l_x1dma"]
    c_ar_map = {}
    acc_free = [None, None]
    s_halo = [None]
    stageslot_free = [None, None]
    stg_free = [[]]
    v_conv0 = {}
    dbc_dmas = {}
    p_win_prev = None
    p_delta_prev = None
    u_out_dma = {}
    z_out_dma = {}
    u_load_dma = {}
    z_load_dma = {}
    zstg_free = [h0[3]["ln2_p1"], h0[3]["ln2_p1"]]
    cstg_free = [None, None]
    last_scan_v = {}
    scan_state = {}
    v_w_last = [None]
    last_prod31 = [None]

    for d in ("f", "b"):
        rev = (d == "b")
        gi0 = 0 if d == "f" else NCH
        if d == "f":
            w_win = [h0[2]["pe_ffn"]] + h0[2]["l_xnd"]
            w_wts = list(l_wt) + [h0[3]["pe_ffn"]] + h0[3]["l_xnd"]
            w_rows = [h0[3]["ln2_p1"]]
            w_misc2 = []
            p_win_prev = [h0[2]["ln2_p1"], h0[2]["l_x1dma"]]
            conv0_extra = [h0[3]["pe_ffn"]]
        else:
            w_win = [p_win_prev]
            w_wts = list(l_wt) + [p_comp_prev]
            w_rows = [p_delta_prev] + list(stg_free[0])
            w_misc2 = [acc_free[1], v_w_last[0]]
            conv0_extra = []
        winT_r = winT[d].rearrange("(k p) m -> k p m", p=128)
        for k in range(8):
            kb.dma(A(R_WIN + k * 512, 512), winT_r[k],
                   waits=(w_win if k < 4 else ()))
        wts_r = wtsp[d].ap().rearrange("(m p) o -> m p o", p=128)
        for m in range(2):
            kb.dma(A(R_WTS + m * 1024, 1024), wts_r[m],
                   waits=(w_wts if m == 0 else ()))
        wx_r = wxT[d].rearrange("(c p) m -> c p m", p=128)
        kb.dma(misc[:, WX0:WX0 + 96], wx_r[0], waits=w_rows)
        kb.dma(misc[:, WX1:WX1 + 96], wx_r[1])
        kb.dma(misc[0:DTR, WDT:WDT + 256], wdtT[d])
        kb.dma(M(WTAP, 8), wconv[d], waits=w_misc2)
        kb.dma(M(BCONV, 2), bconv[d])
        kb.dma(M(BDT, 2), bdt[d])
        kb.dma(M(DPC, 2), dpv[d])
        kb.dma(misc[:, ALOG:ALOG + 32], alog[d])
        l_wd = kb.dma_all()
        s_a = kb.sact(M(A_SB, 32), misc[:, ALOG:ALOG + 32], AF.Exp, waits=l_wd)
        kb.vts(M(BDT, 2), M(BDT, 2), -1.0, ALU.mult, waits=l_wd)
        kb.vmemset(M(HCAR, 32), 0.0, waits=[s_a])
        kb.vmemset(arena[:, R_UIN:R_UIN + 4], 0.0)
        v_init = kb.vmemset(arena[:, R_UIN + 2052:R_UIN + 2056], 0.0)

        s_ev_cur = None

        def stageA_ts(c, ts):
            nonlocal p_win_prev, s_ev_cur
            c8 = c % NCH
            if not rev:
                srcx = xn_blk[ts][c8]
                c_xn = c_xn_ts[ts]
            else:
                srcx = xn_blk[NTS - 1 - ts][7 - c8]
                c_xn = c_xn_ts[NTS - 1 - ts]
            w = [c_xn]
            if p_win_prev:
                w.extend(p_win_prev if isinstance(p_win_prev, list)
                         else [p_win_prev])
            l_xt = kb.dma(A(R_XT, 4096), (srcx).transpose([1, 0, 2]), waits=w,
                          lane=2)
            slot = c % 2
            for m in range(4):
                bank = m
                mov = [A(R_XT + k * 512, 512) for k in range(8)]
                if rev:
                    mov = [t[:, ::-1] for t in mov]
                mats = [(A(R_WIN + k * 512 + m * 128, 128), mov[k])
                        for k in range(8)]
                dn = kb.mm_group(bank, mats, waits=[l_xt])
                if m < 2:
                    kb.vcopy(arena[:, R_UIN + m * 2052 + 4 + ts * TSZ:
                                   R_UIN + m * 2052 + 4 + ts * TSZ + 512],
                             ps[bank][:, 0:512], waits=[dn])
                else:
                    db = m - 2
                    zslot = db
                    wv = [dn]
                    if zstg_free[zslot]:
                        wv.append(zstg_free[zslot])
                    s_z = kb.sact(arena[:, R_ZSTG + zslot * 512:R_ZSTG + zslot * 512 + 512],
                                  ps[bank][:, 0:512], AF.Silu, waits=wv)
                    zw = [s_z]
                    if (c - 2) in z_load_dma:
                        zw += z_load_dma[c - 2]
                    dma_t = kb.dma(zsp_r[slot][db][:, ts * TSZ:(ts + 1) * TSZ],
                                   arena[:, R_ZSTG + zslot * 512:R_ZSTG + zslot * 512 + 512],
                                   waits=zw, lane=2)
                    zstg_free[zslot] = dma_t
                    z_out_dma.setdefault(c, []).append(dma_t)
                kb.consume_bank(bank, "scalar" if m >= 2 else "vector")
            p_win_prev = now(P)
            s_ev_cur = now(V)
            slot_a = ts % 2
            for db in range(2):
                base = R_UIN + db * 2052 + 4 + ts * TSZ
                acc = arena[:, R_DBC + slot_a * 1024 + db * 512:
                            R_DBC + slot_a * 1024 + db * 512 + 512]
                wv = []
                if db == 0:
                    if acc_free[slot_a] is not None:
                        wv.append(acc_free[slot_a])
                    if stageslot_free[slot_a] is not None:
                        wv.append(stageslot_free[slot_a])
                    if ts == 0:
                        wv += [v_init]
                        if p_delta_prev:
                            wv.append(p_delta_prev)
                        if s_halo[0] is not None:
                            wv.append(s_halo[0])
                        if c8 == 0:
                            wv += conv0_extra
                kb.vts(acc, arena[:, base - 3:base - 3 + 512],
                       M(WTAP + db * 4), ALU.mult, waits=wv)
                for j in range(1, 4):
                    kb.vstt(acc, arena[:, base - 3 + j:base - 3 + j + 512],
                            M(WTAP + db * 4 + j), acc, ALU.mult, ALU.add)
            v_cv = now(V)
            if ts == 0:
                v_conv0[c] = v_cv
            s_silu = None
            for db in range(2):
                acc = arena[:, R_DBC + slot_a * 1024 + db * 512:
                            R_DBC + slot_a * 1024 + db * 512 + 512]
                sw = [v_cv] if db == 0 else []
                if db == 0:
                    sw += stg_free[0]
                s_silu = kb.sact(A(R_STG + db * 512, 512), acc, AF.Silu,
                                 bias=M(BCONV + db), waits=sw)
            acc_free[slot_a] = s_silu
            spl = []
            for db in range(2):
                uw = [s_silu]
                if (c - 2) in u_load_dma:
                    uw += u_load_dma[c - 2]
                dt_ = kb.dma(usp_r[slot][db][:, ts * TSZ:(ts + 1) * TSZ],
                             A(R_STG + db * 512, 512), waits=uw, lane=2)
                spl.append(dt_)
            u_out_dma.setdefault(c, []).extend(spl)
            mats = [(misc[:, (WX0 if kk == 0 else WX1):
                           (WX0 if kk == 0 else WX1) + 96],
                     A(R_STG + kk * 512, 512)) for kk in range(2)]
            dn4 = kb.mm_group(4, mats, waits=[s_silu] + list(l_wd),
                              out_rows=(0, 96))
            stg_free[0] = [dn4] + spl
            sd = R_DBC + ((ts + 1) % 2) * 1024
            s_dbc = kb.scopy(arena[0:96, sd:sd + 512], ps[4][0:96, 0:512],
                             waits=[dn4])
            kb.consume_bank(4, "scalar")
            l_blk = kb.dma(dbc_in[d][c8].ap()[:, ts * TSZ:(ts + 1) * TSZ],
                           arena[0:96, sd:sd + 512], waits=[s_dbc], lane=2)
            stageslot_free[(ts + 1) % 2] = l_blk
            dbc_dmas.setdefault(c, []).append(l_blk)

        def stageA_tail(c):
            for db in range(2):
                base = R_UIN + db * 2052
                s_halo[0] = kb.scopy(arena[:, base + 1:base + 4],
                                     arena[:, base + 2049:base + 2052],
                                     waits=([v_conv0[c]] if db == 0 else ()))
            c8 = c % NCH
            c_ar_map[c] = kb.coll("AllReduce", ALU.add,
                                  [dbc_in[d][c8].ap().opt()],
                                  [dbc_out[d][c8].ap().opt()],
                                  waits=dbc_dmas[c])

        def bcast_src(c, n, which):
            row = 64 + n if which == "B" else 80 + n
            return dbc_out[d][c % NCH].ap()[row:row + 1, :].unsqueeze(0) \
                .broadcast_to([128, 1, TOK])

        def scan_prologue(c):
            nonlocal p_delta_prev
            st = {"v_scan": {}, "g_row": {}, "v_prod": {}, "l_bc": {}}
            scan_state[c] = st
            c_ar = c_ar_map[c]
            l_dt = kb.dma(arena[0:DTR, R_DBC:R_DBC + 2048],
                          dbc_out[d][c % NCH].ap()[0:DTR, :], waits=[c_ar],
                          lane=3)
            wb = [c_ar]
            if (c - 1) in last_scan_v:
                wb.append(last_scan_v[c - 1])
            b1 = kb.dma(A(R_BB, 2048).rearrange("p (g n) -> p g n", g=1),
                        bcast_src(c, 0, "B"), waits=wb, lane=0)
            b2 = kb.dma(A(R_CB, 2048).rearrange("p (g n) -> p g n", g=1),
                        bcast_src(c, 0, "C"), waits=wb, lane=1)
            st["l_bc"][0] = [b1, b2]
            uw = list(u_out_dma[c])
            if (c - 1) in last_scan_v:
                uw.append(last_scan_v[c - 1])
            lu = [kb.dma(A(R_UW + db * 2048, 2048), usp_r[c % 2][db], waits=uw,
                         lane=3) for db in range(2)]
            u_load_dma[c] = lu
            for m in range(2):
                for ns in range(4):
                    bank = m
                    dn = kb.mm_group(bank,
                                     [(misc[0:DTR, WDT + m * 128:WDT + (m + 1) * 128],
                                       arena[0:DTR, R_DBC + ns * 512:R_DBC + (ns + 1) * 512])],
                                     waits=[l_dt])
                    wv = [dn]
                    if m == 0 and ns == 0 and (c - 1) in last_scan_v:
                        wv.append(last_scan_v[c - 1])
                    kb.sact(A(R_DA + ns * 512, 512), ps[bank][:, 0:512],
                            AF.Sigmoid, bias=M(BDT + m),
                            scale=M(CNEG1), waits=wv)
                    kb.consume_bank(bank, "scalar")
                s_dl = None
                for ns in range(4):
                    s_dl = kb.sact(A(R_DL + m * 2048 + ns * 512, 512),
                                   A(R_DA + ns * 512, 512), AF.Ln)
                st.setdefault("s_dl", {})[m] = s_dl
            st["lu"] = lu
            p_delta_prev = now(P)

        def scan_prologue_b(c):
            st = scan_state[c]
            st_prev = scan_state.get(c - 1)
            st["v_w_db"] = {}
            for db in range(2):
                wv = [st["lu"][db], st["s_dl"][db]]
                if db == 0 and st_prev is not None:
                    wv.append(st_prev["v_yg"])
                st.setdefault("v_init_db", {})[db] = kb.vts(
                    A(R_YACC + db * 2048, 2048), A(R_UW + db * 2048, 2048),
                    M(DPC + db), ALU.mult, waits=wv)
                st["v_w_db"][db] = kb.vstt(A(R_UW + db * 2048, 2048),
                                           A(R_DL + db * 2048, 2048),
                                           -1.0, A(R_UW + db * 2048, 2048),
                                           ALU.mult, ALU.mult)
            st["v_w"] = now(V)
            v_w_last[0] = st["v_w"]

        def scan_rows(c, n_list):
            st = scan_state[c]
            v_scan, g_row, v_prod, l_bc = (st["v_scan"], st["g_row"],
                                           st["v_prod"], st["l_bc"])
            for n in n_list:
                buf = n % 2
                if n + 1 < DS and (n + 1) not in l_bc:
                    nbuf = (n + 1) % 2
                    ww = []
                    if n == 0 and last_prod31[0] is not None:
                        ww.append(last_prod31[0])
                    if n >= 1:
                        rp = 2 * n - 1
                        ww = [g_row[rp]]
                        ww.append(v_prod[rp])
                    b1 = kb.dma(A(R_BB + nbuf * 2048, 2048)
                                .rearrange("p (g nn) -> p g nn", g=1),
                                bcast_src(c, n + 1, "B"), waits=ww, lane=0)
                    b2 = kb.dma(A(R_CB + nbuf * 2048, 2048)
                                .rearrange("p (g nn) -> p g nn", g=1),
                                bcast_src(c, n + 1, "C"), waits=ww, lane=1)
                    l_bc[n + 1] = [b1, b2]
                for db in range(2):
                    row = n * 2 + db
                    ch = db * 16 + n
                    wda = [st["v_w"]] if row < 1 else [v_scan[row - 1]]
                    s_da = kb.sact(A(R_DA, 2048), A(R_DL + db * 2048, 2048),
                                   AF.Exp, scale=M(A_SB + ch), waits=wda)
                    kb.vtt(A(R_DBU, 2048), A(R_UW + db * 2048, 2048),
                           A(R_BB + buf * 2048, 2048), ALU.mult,
                           waits=l_bc[n])
                    v_scan[row] = kb.vscan(A(R_H, 2048), A(R_DA, 2048),
                                           A(R_DBU, 2048), M(HCAR + ch),
                                           waits=[s_da])
                    kb.vcopy(M(HCAR + ch), arena[:, R_H + 2047:R_H + 2048])
                    v_prod[row] = kb.vtt(A(R_PROD, 2048), A(R_H, 2048),
                                         A(R_CB + buf * 2048, 2048), ALU.mult,
                                         waits=l_bc[n])
                    g_row[row] = kb.vtt(A(R_YACC + db * 2048, 2048),
                                        A(R_YACC + db * 2048, 2048),
                                        A(R_PROD, 2048), ALU.add)
            if (DS - 1) in n_list:
                last_scan_v[c] = v_scan[2 * DS - 1]
                last_prod31[0] = v_prod[2 * DS - 1]

        def gate_q(c):
            nonlocal v_yg_prev, p_comp_prev
            st = scan_state[c]
            g_row, v_scan = st["g_row"], st["v_scan"]
            slot = c % 2
            v_yg = None
            for db in range(2):
                zw = list(z_out_dma[c]) + [v_scan[2 * DS - 1]]
                if db == 1:
                    zw.append(v_yg)
                lz = kb.dma(A(R_DBU, 2048), zsp_r[slot][db], waits=zw, lane=3)
                z_load_dma.setdefault(c, []).append(lz)
                wv = [lz, g_row[2 * (DS - 1) + db]]
                if db == 0 and p_comp_prev:
                    wv.append(p_comp_prev)
                v_yg = kb.vtt(A(R_UWSTG + db * 2048, 2048),
                              A(R_YACC + db * 2048, 2048),
                              A(R_DBU, 2048), ALU.mult, waits=wv)
            st["v_yg"] = v_yg
            v_yg_prev = v_yg

        def composite_tasks(c):
            st = scan_state[c]
            for ts2 in range(NTS):
                for m in range(8):
                    yield (c, ts2, m, st)

        comp_bank = [0]

        def emit_comp(task):
            nonlocal p_comp_prev
            c, ts2, m, st = task
            c8 = c % NCH
            bank = 5 + comp_bank[0] % 3
            comp_bank[0] += 1
            mats = [(A(R_WTS + kk * 1024 + m * 128, 128),
                     A(R_UWSTG + kk * 2048 + ts2 * TSZ, 512))
                    for kk in range(2)]
            dn = kb.mm_group(bank, mats, waits=[st["v_yg"]])
            cslot = (ts2 * 8 + m) % 2
            wv = [dn]
            if cstg_free[cslot]:
                wv.append(cstg_free[cslot])
            dst = arena[:, R_CSTG + cslot * 512:R_CSTG + cslot * 512 + 512]
            if rev:
                dst = dst[:, ::-1]
            s_pe = kb.scopy(dst, ps[bank][:, 0:512], waits=wv)
            kb.consume_bank(bank, "scalar")
            if not rev:
                sl = ts2
                dma_t = kb.dma(
                    rsin_r[sl][c8][m],
                    arena[:, R_CSTG + cslot * 512:R_CSTG + cslot * 512 + 512],
                    waits=[s_pe], lane=3)
            else:
                blk = 7 - c8
                sl = NTS - 1 - ts2
                dma_t = kb.dma(
                    rsin_r[sl][blk][m],
                    arena[:, R_CSTG + cslot * 512:R_CSTG + cslot * 512 + 512],
                    waits=[s_pe] + fwd_block_written[blk],
                    eng="gpsimd", accum=True)
                if c8 == NCH - 1 and m == 7:
                    c_rs4[sl] = kb.coll(
                        "ReduceScatter", ALU.add, [rs_in[sl].ap().opt()],
                        [rs_out[sl].ap().opt()], waits=[dma_t])
            cstg_free[cslot] = dma_t
            if ts2 == NTS - 1 and m == 7:
                if not rev:
                    fwd_block_written[c8] = kb.dma_all()
                p_comp_prev = now(P)
                del scan_state[c]

        ROW_SLOTS = {0: range(0, 2), 1: range(2, 5), 2: range(5, 8),
                     3: range(8, 11)}
        pending_comp = []

        def drain_comp(k):
            for _ in range(k):
                if pending_comp:
                    emit_comp(pending_comp.pop(0))

        def scan_rows_drip(c, n_list):
            for n in n_list:
                scan_rows(c, [n])
                drain_comp(2)

        for i in range(NCH + 1):
            if i >= 1:
                scan_prologue(gi0 + i - 1)
            for ts in range(NTS):
                if i < NCH:
                    stageA_ts(gi0 + i, ts)
                if i >= 1:
                    if ts == 1:
                        scan_prologue_b(gi0 + i - 1)
                        scan_rows_drip(gi0 + i - 1, range(0, 3))
                    elif ts == 2:
                        scan_rows_drip(gi0 + i - 1, range(3, 6))
                    elif ts == 3:
                        scan_rows_drip(gi0 + i - 1, range(6, 9))
            if i < NCH:
                stageA_tail(gi0 + i)
            if i >= 1:
                scan_rows_drip(gi0 + i - 1, range(9, DS))
                drain_comp(99)
                gate_q(gi0 + i - 1)
                pending_comp = list(composite_tasks(gi0 + i - 1))
        drain_comp(99)


    p3_bar = [now(S), now(V), now(P), now(G), (GD, GD.ctr)] + kb.dma_all()
    for k in range(8):
        kb.dma(A(R_WEFF + k * 1024, 1024), wsp_r[k],
               waits=(p3_bar if k < 4 else ()))
    l_weff2 = kb.dma_all()
    rs_r = [r.ap().rearrange("(k p) t -> k p t", p=128) for r in rs_out]
    out_r = out_t.rearrange("(k p) t -> k p t", p=128)
    P3X = (R_XT, R_CB)
    P3XN = (R_WIN, R_DA)
    P3RS = (R_UWSTG, R_H)
    h3 = {}
    TORD = list(range(NTS - 1, -1, -1))
    for it in range(NTS + 1):
        ts = TORD[it] if it < NTS else None
        pv = TORD[it - 1] if it >= 1 else None
        if ts is not None:
            p = it % 2
            cs = slice(ts * TSZ, (ts + 1) * TSZ)
            w = ([h3[TORD[it - 2]]["ln_p1"], h3[TORD[it - 2]]["l_outdma"]]
                 if it >= 2 else [])
            l_x1 = kb.dma(A(P3X[p], 4096),
                          (x1_r[:, :, cs]).transpose([1, 0, 2]), waits=w)
            wt = [c_rs4[ts]] + ([h3[TORD[it - 2]]["v_t"]] if it >= 2 else [])
            l_t = kb.dma(A(P3RS[p], 4096),
                         (rs_r[ts]).transpose([1, 0, 2]),
                         waits=wt)
            for pb in range(8):
                kb.vts(A(R_TMP + pb * 512, 512), A(P3RS[p] + pb * 512, 512),
                       M(BPROJ + pb), ALU.add, waits=([l_t] if pb == 0 else ()))
            v_t = now(V)
            evs_x2 = []
            for pb in range(8):
                ev = kb.vtt(A(P3X[p] + pb * 512, 512),
                            A(R_TMP + pb * 512, 512),
                            A(P3X[p] + pb * 512, 512), ALU.add,
                            waits=([l_x1] if pb == 0 else ()))
                evs_x2.append(ev)
            h3[ts] = {"l_x1": l_x1, "v_t": v_t, "evs_x2": evs_x2}
        if pv is not None:
            p = (it - 1) % 2
            cs = slice(pv * TSZ, (pv + 1) * TSZ)
            v_x3, _, pe_ffn = ffn_apply(P3XN[p], P3X[p], P3X[p],
                                        h3[pv]["v_xn3"], h3[pv]["evs_x2"][-1],
                                        stat_ready=l_weff2)
            l_outdma = kb.dma(out_r[:, :, cs].transpose([1, 0, 2]),
                              A(P3X[p], 4096), waits=[v_x3])
            h3[pv].update(pe_ffn=pe_ffn, l_outdma=l_outdma)
        if ts is not None:
            p = it % 2
            dg = [h3[TORD[it - 2]]["pe_ffn"]] if it >= 2 else None
            v_xn3, ln_p1 = layer_norm(P3X[p], P3XN[p], h3[ts]["evs_x2"],
                                      dst_guard=dg)
            h3[ts].update(v_xn3=v_xn3, ln_p1=ln_p1)
    final_waits = kb.dma_all()
    kb.op("sync", lambda e: [e.wait_ge(sm.h, v) for sm, v in final_waits],
          sem=kb.Ls[0], incn=0)

    return nc, kb


def _emit(nc, kb):
    import contextlib
    with contextlib.ExitStack() as st:
        for s in kb.sems:
            s.h = st.enter_context(nc.semaphore(s.name))
        blk = st.enter_context(nc.Block())
        blk.sync(lambda e: kb.emit(e, "sync"))
        blk.scalar(lambda e: kb.emit(e, "scalar"))
        blk.vector(lambda e: kb.emit(e, "vector"))
        blk.tensor(lambda e: kb.emit(e, "tensor"))
        blk.gpsimd(lambda e: kb.emit(e, "gpsimd"))
    return nc


def _host_inputs(c, a):
    t0, t1 = c * TOK, (c + 1) * TOK
    d0, d1 = c * DSL, (c + 1) * DSL
    ct = np.ascontiguousarray
    m = {}
    m["x_fm"] = ct(a["L0_attn_x"][0, t0:t1, :].T)
    for d, sfx in (("f", "_f"), ("b", "_b")):
        winT = a["L0_attn_win" + sfx].T
        m[f"winT_{d}"] = ct(np.concatenate(
            [winT[:, d0:d1], winT[:, DI + d0:DI + d1]], axis=1))
        m[f"wxT_{d}"] = ct(a["L0_attn_wx" + sfx].T[d0:d1, :])
        m[f"wdtT_{d}"] = ct(a["L0_attn_wdt" + sfx].T[:, d0:d1])
        m[f"wconv_{d}"] = ct(a["L0_attn_wconv" + sfx][d0:d1, 0, :]
                             .reshape(2, 128, 4).transpose(1, 0, 2).reshape(128, 8))
        m[f"bconv_{d}"] = ct(a["L0_attn_bconv" + sfx][d0:d1].reshape(2, 128).T)
        m[f"bdt_{d}"] = ct(a["L0_attn_bdt" + sfx][d0:d1].reshape(2, 128).T)
        m[f"alog_{d}"] = ct(a["L0_attn_Alog" + sfx][d0:d1]
                            .reshape(2, 128, DS).transpose(1, 0, 2).reshape(128, 32))
        m[f"dp_{d}"] = ct(a["L0_attn_Dp" + sfx][d0:d1].reshape(2, 128).T)
        m[f"woutc_{d}"] = ct(a["L0_attn_wout" + sfx][:, d0:d1])
        half = a["L0_attn_wproj"][:, :DM] if d == "f" else a["L0_attn_wproj"][:, DM:]
        m[f"wprojT_{d}"] = ct(half.T)
    m["w_T"] = ct(a["L0_ffn_W"].T)
    m["v_rT"] = ct(a["L0_ffn_V_r"].T)
    m["c_T"] = ct(a["L0_ffn_C"].T)
    m["b_ffn"] = ct(a["L0_ffn_b"].reshape(8, 128).T)
    m["bproj_v"] = ct(a["L0_attn_bproj"].reshape(8, 128).T)
    return m


def kernel(**inputs):
    a = {k: np.asarray(v, dtype=np.float32) for k, v in inputs.items()}
    nc, kb = build_program()
    _emit(nc, kb)
    in_maps = [_host_inputs(c, a) for c in range(NCORES)]
    res = run_bass_kernel_spmd(nc, in_maps, core_ids=list(range(NCORES)))
    outs = [np.asarray(res.results[c]["out"]).T for c in range(NCORES)]
    return np.concatenate(outs, axis=0).reshape(1, SEQ, DM)


# version 20
# speedup vs baseline: 1.0435x; 1.0308x over previous
import numpy as np
import concourse.bass as bass
import concourse.mybir as mybir
from concourse.bass_utils import run_bass_kernel_spmd

F32 = mybir.dt.float32
AF = mybir.ActivationFunctionType
ALU = mybir.AluOpType

NCORES = 8
DM = 1024
DI = 2048
DS = 16
DTR = 64
SEQ = 16384
TOK = SEQ // NCORES
DSL = DI // NCORES
NTS = 4
TSZ = TOK // NTS
NCH = 8
EPS = 1e-5


class Sem:
    def __init__(self, name):
        self.name, self.h, self.ctr = name, None, 0


class KB:
    ENGS = ("sync", "scalar", "vector", "tensor", "gpsimd")

    def __init__(self, nc):
        self.nc = nc
        self.streams = {e: [] for e in self.ENGS}
        self.Ls = [Sem(f"L{i}") for i in range(4)]
        self.lrot = 0
        self.S = Sem("S")
        self.V = Sem("V")
        self.P = Sem("P")
        self.G = Sem("G")
        self.GD = Sem("GD")
        self.GA = Sem("GA")
        self.C = Sem("C")
        self.sems = self.Ls + [self.S, self.V, self.P, self.G, self.GD, self.GA,
                               self.C]
        self.esem = {"scalar": self.S, "vector": self.V,
                     "tensor": self.P, "gpsimd": self.G}
        self.bank_free = [None] * 8

    def dma_all(self):
        return [(s, s.ctr) for s in self.Ls if s.ctr]

    def op(self, eng, fn, waits=(), sem=None, incn=1):
        s = sem if sem is not None else self.esem[eng]
        waits = list(waits)
        if eng in ("scalar", "vector", "gpsimd") and s is self.esem.get(eng) and s.ctr:
            waits.append((s, s.ctr))
        s.ctr += incn
        waits = tuple((sm, v) for (sm, v) in waits if isinstance(sm, Sem) and v > 0)
        self.streams[eng].append((fn, waits, s, incn))
        return (s, s.ctr)

    def dma(self, out, in_, waits=(), eng="sync", accum=False, lane=None,
            gsem=None):
        if eng == "gpsimd":
            s = gsem if gsem is not None else self.GD
        else:
            if lane is None:
                s = self.Ls[self.lrot]
                self.lrot = (self.lrot + 1) % len(self.Ls)
            else:
                s = self.Ls[lane]
        if s.ctr:
            waits = list(waits) + [(s, s.ctr)]
        if accum:
            return self.op(eng, lambda e, o=out, i=in_:
                           e.dma_start(out=o, in_=i, accum_op=ALU.add),
                           waits=waits, sem=s, incn=16)
        return self.op(eng, lambda e, o=out, i=in_: e.dma_start(out=o, in_=i),
                       waits=waits, sem=s, incn=16)

    def mm_group(self, bank, mats, ncols=512, waits=(), out_rows=None):
        w = list(waits)
        if self.bank_free[bank] is not None:
            w.append(self.bank_free[bank])
        n = len(mats)
        last = None
        for j, (lhsT, rhs) in enumerate(mats):
            o = self.ps[bank][:, 0:ncols] if out_rows is None else \
                self.ps[bank][out_rows[0]:out_rows[1], 0:ncols]
            last = self.op("tensor",
                           lambda e, oo=o, a=lhsT, r=rhs, st=(j == 0), sp=(j == n - 1):
                           e.matmul(oo, a, r, start=st, stop=sp),
                           waits=(w if j == 0 else ()))
        return last

    def consume_bank(self, bank, eng):
        self.bank_free[bank] = (self.esem[eng], self.esem[eng].ctr)

    def sact(self, out, in_, func, bias=0.0, scale=1.0, waits=()):
        return self.op("scalar", lambda e, o=out, i=in_, f=func, bb=bias, ss=scale:
                       e.activation(o, i, f, bias=bb, scale=ss), waits=waits)

    def scopy(self, out, in_, waits=()):
        return self.op("scalar", lambda e, o=out, i=in_: e.copy(o, i), waits=waits)

    def vtt(self, out, i0, i1, opk, waits=()):
        return self.op("vector", lambda e, o=out, a=i0, c=i1, k=opk:
                       e.tensor_tensor(o, a, c, k), waits=waits)

    def vstt(self, out, i0, scal, i1, op0, op1, waits=()):
        return self.op("vector", lambda e, o=out, a=i0, s=scal, c=i1, k0=op0, k1=op1:
                       e.scalar_tensor_tensor(o, a, s, c, k0, k1), waits=waits)

    def vts(self, out, i0, scal, opk, waits=()):
        return self.op("vector", lambda e, o=out, a=i0, s=scal, k=opk:
                       e.tensor_single_scalar(o, a, s, k), waits=waits)

    def vts2(self, out, i0, s1, s2, op0, op1, waits=()):
        return self.op("vector", lambda e, o=out, a=i0, x=s1, y=s2, k0=op0, k1=op1:
                       e.tensor_scalar(o, a, x, y, k0, k1), waits=waits)

    def vcopy(self, out, in_, waits=()):
        return self.op("vector", lambda e, o=out, i=in_: e.tensor_copy(o, i),
                       waits=waits)

    def gtt(self, out, i0, i1, opk, waits=()):
        return self.op("gpsimd", lambda e, o=out, a=i0, c=i1, k=opk:
                       e.tensor_tensor(o, a, c, k), waits=waits)

    def vscan(self, out, d0, d1, init, waits=()):
        return self.op("vector", lambda e, o=out, a=d0, c=d1, ii=init:
                       e.tensor_tensor_scan(o, a, c, ii, ALU.mult, ALU.add),
                       waits=waits)

    def vmemset(self, ap, val, waits=()):
        return self.op("vector", lambda e, a=ap, v=val: e.memset(a, v), waits=waits)

    def vrecip(self, out, in_, waits=()):
        return self.op("vector", lambda e, o=out, i=in_: e.reciprocal(o, i),
                       waits=waits)

    def coll(self, kind, opk, ins, outs, waits=()):
        return self.op("gpsimd", lambda e, k=kind, o=opk, i=ins, u=outs:
                       e.collective_compute(k, o, replica_groups=[list(range(NCORES))],
                                            ins=i, outs=u),
                       waits=waits, sem=self.C)

    def emit(self, eng_obj, eng_name):
        for fn, waits, sem, incn in self.streams[eng_name]:
            for (sm, val) in waits:
                eng_obj.wait_ge(sm.h, val)
            ins = fn(eng_obj)
            if incn:
                ins.then_inc(sem.h, incn)


def build_program():
    nc = bass.Bass("TRN2", target_bir_lowering=False)

    def inp(name, shape):
        return nc.dram_tensor(name, list(shape), F32, kind="ExternalInput").ap()

    x_fm = inp("x_fm", (DM, TOK))
    winT = {d: inp(f"winT_{d}", (DM, 2 * DSL)) for d in "fb"}
    wxT = {d: inp(f"wxT_{d}", (DSL, 96)) for d in "fb"}
    wdtT = {d: inp(f"wdtT_{d}", (DTR, DSL)) for d in "fb"}
    wconv = {d: inp(f"wconv_{d}", (128, 8)) for d in "fb"}
    bconv = {d: inp(f"bconv_{d}", (128, 2)) for d in "fb"}
    bdt = {d: inp(f"bdt_{d}", (128, 2)) for d in "fb"}
    alog = {d: inp(f"alog_{d}", (128, 32)) for d in "fb"}
    dpv = {d: inp(f"dp_{d}", (128, 2)) for d in "fb"}
    woutc = {d: inp(f"woutc_{d}", (DM, DSL)) for d in "fb"}
    wprojT = {d: inp(f"wprojT_{d}", (DM, DM)) for d in "fb"}
    w_T = inp("w_T", (DM, DM))
    v_rT = inp("v_rT", (32, DM))
    c_T = inp("c_T", (32, DM))
    b_ffn = inp("b_ffn", (128, 8))
    bproj_v = inp("bproj_v", (128, 8))

    out_t = nc.dram_tensor("out", [DM, TOK], F32, kind="ExternalOutput").ap()

    def idram(name, shape):
        return nc.dram_tensor(name, list(shape), F32)

    x1sp = idram("x1sp", (DM, TOK))
    weffsp = idram("weffsp", (DM, DM))
    wtsp = {d: idram(f"wtsp_{d}", (2 * 128, DM)) for d in "fb"}
    xn_in = [idram(f"xn_in{t}", (DM, TSZ)) for t in range(NTS)]
    xn_out = [nc.dram_tensor(f"xn_out{t}", [NCORES * DM, TSZ], F32,
                             addr_space="Shared") for t in range(NTS)]
    dbc_in = {d: [idram(f"dbcin_{d}{c}", (96, TOK)) for c in range(NCH)] for d in "fb"}
    dbc_out = {d: [idram(f"dbcout_{d}{c}", (96, TOK)) for c in range(NCH)]
               for d in "fb"}
    rs_in = [idram(f"rs_in{h}", (NCORES * DM, TSZ)) for h in range(NTS)]
    rs_out = [idram(f"rs_out{h}", (DM, TSZ)) for h in range(NTS)]
    u_sp = idram("u_sp", (3 * 2 * 128, TOK))
    z_sp = idram("z_sp", (3 * 2 * 128, TOK))

    ARENA = 52232
    arena = nc.alloc_sbuf_tensor("arena", [128, ARENA], F32).ap()
    misc = nc.alloc_sbuf_tensor("misc", [128, 800], F32).ap()

    A_SB, WTAP, BCONV, BDT, DPC = 0, 32, 40, 42, 44
    BFFN, HALFB, BPROJ, ONESC, HCAR = 46, 54, 62, 70, 72
    ROWS, ONESR = 104, 640
    CEPS, CINV, CHALF, CNEG1 = 768, 769, 770, 771
    WX0, WX1, WDT, ALOG = ROWS, ROWS + 96, ROWS + 192, ROWS + 448

    def M(o, n=1):
        return misc[:, o:o + n]

    off = [0]

    def alloc(n):
        a = off[0]
        off[0] += n
        assert off[0] <= ARENA, f"arena overflow {off[0]}"
        return a

    R_XT = alloc(4096)
    R_WIN = alloc(4096)
    R_UIN = alloc(4104)
    R_UWSTG = alloc(4096)
    R_ZSTG = alloc(1024)
    R_DBC = alloc(2048)
    R_DL = alloc(4096)
    R_UW = alloc(4096)
    R_BB = alloc(4096)
    R_CB = alloc(4096)
    R_DA = alloc(2048)
    R_DBU = alloc(2048)
    R_H = alloc(2048)
    R_PROD = alloc(2048)
    R_YACC = alloc(4096)
    R_WTS = alloc(2048)
    R_CSTG = alloc(1024)
    R_STG = alloc(1024)
    P1_END = off[0]
    R_X, R_XN, R_TMP, R_SQ = R_XT, R_WIN, R_UIN, R_DL
    R_WEFF = R_UW
    R_WOC = R_CB
    R_WPT = R_YACC
    R_WPT = R_DA
    R_RS = R_UWSTG
    R_CMP = R_YACC
    R_SCR = R_ZSTG
    def A(o, n):
        return arena[:, o:o + n]

    ps = [nc.alloc_psum_tensor(f"ps{i}", [128, 512], F32).ap() for i in range(8)]
    kb = KB(nc)
    kb.ps = ps
    S, V, P, G, GD, C = kb.S, kb.V, kb.P, kb.G, kb.GD, kb.C

    def now(sem):
        return (sem, sem.ctr)

    kb.vmemset(M(ONESC), 1.0)
    kb.vmemset(misc[0:1, ONESR:ONESR + 128], 1.0)
    kb.vmemset(M(CEPS), EPS)
    kb.vmemset(M(CINV), 1.0 / DM)
    kb.vmemset(M(CHALF), 0.5)
    kb.vmemset(M(CNEG1), -1.0)
    l_b1 = kb.dma(M(BFFN, 8), b_ffn)
    l_b2 = kb.dma(M(BPROJ, 8), bproj_v)
    kb.vts(M(HALFB, 8), M(BFFN, 8), 0.5, ALU.mult, waits=[l_b1])

    x_r0 = x_fm.rearrange("(k p) t -> k p t", p=128)
    pre_lx = {}
    for ts0 in range(2):
        dst0 = (R_XT, R_UWSTG)[ts0]
        pre_lx[ts0] = kb.dma(A(dst0, 4096),
                             (x_r0[:, :, ts0 * TSZ:(ts0 + 1) * TSZ])
                             .transpose([1, 0, 2]))
    wT_r = w_T.rearrange("(k p) o -> k p o", p=128)
    l_wT = None
    for k in range(8):
        l_wT = kb.dma(A(R_WEFF + k * 1024, 1024), wT_r[k])
    l_v1 = kb.dma(arena[0:32, R_TMP:R_TMP + DM], v_rT)
    l_v2 = kb.dma(arena[0:32, R_TMP + DM:R_TMP + 2 * DM], c_T)
    w_all = kb.dma_all()
    for k in range(8):
        for hf in range(2):
            bank = hf
            dn = kb.mm_group(bank,
                             [(arena[0:32, R_TMP + k * 128:R_TMP + (k + 1) * 128],
                               arena[0:32, R_TMP + DM + hf * 512:R_TMP + DM + (hf + 1) * 512])],
                             waits=w_all)
            dst = A(R_WEFF + k * 1024 + hf * 512, 512)
            kb.vtt(dst, dst, ps[bank][:, 0:512], ALU.add, waits=[dn])
            kb.consume_bank(bank, "vector")
    v_weff = now(V)
    wsp_r = weffsp.ap().rearrange("(k p) o -> k p o", p=128)
    l_weffsp = None
    for k in range(8):
        l_weffsp = kb.dma(wsp_r[k], A(R_WEFF + k * 1024, 1024), waits=[v_weff])

    for d in "fb":
        wc_r = woutc[d].rearrange("(k p) i -> k p i", p=128)
        wp_r = wprojT[d].rearrange("(k p) o -> k p o", p=128)
        for k in range(8):
            kb.dma(A(R_WOC + k * 256, 256), wc_r[k])
            kb.dma(A(R_WPT + k * 1024, 1024), wp_r[k])
        w_all = kb.dma_all()
        for m in range(2):
            for hf in range(2):
                bank = 2 + hf
                mats = [(A(R_WOC + k * 256 + m * 128, 128),
                         A(R_WPT + k * 1024 + hf * 512, 512)) for k in range(8)]
                dn = kb.mm_group(bank, mats, waits=w_all)
                kb.scopy(A(R_CMP + m * 1024 + hf * 512, 512), ps[bank][:, 0:512],
                         waits=[dn])
                kb.consume_bank(bank, "scalar")
        s_cmp = now(S)
        wt_r = wtsp[d].ap().rearrange("(m p) o -> m p o", p=128)
        for m in range(2):
            kb.dma(wt_r[m], A(R_CMP + m * 1024, 1024), waits=[s_cmp])
    l_wt = kb.dma_all()

    def BO(b, i):
        return b + i * 512 if isinstance(b, int) else b[i]

    def layer_norm(src, dst, ready, dst_guard=None, sq_guard=None):
        rlist = ready if isinstance(ready, list) else None
        sMean = arena[0:1, R_SCR:R_SCR + 512]
        sMsq = arena[0:1, R_SCR + 512:R_SCR + 1024]
        sTmp = misc[0:1, ROWS:ROWS + 512]
        for pb in range(8):
            w = []
            if rlist is not None:
                w.append(rlist[pb])
            elif pb == 0:
                w.append(ready)
            if pb == 0 and sq_guard is not None:
                w.append(sq_guard)
            kb.sact(A(R_SQ + pb * 512, 512), A(src + pb * 512, 512), AF.Square,
                    waits=w)
        s_sq = now(S)
        for pb in range(8):
            w0 = []
            if rlist is not None:
                w0.append(rlist[pb])
            elif pb == 0:
                w0.append(ready)
            if pb == 0 and kb.bank_free[5]:
                w0.append(kb.bank_free[5])
            kb.op("tensor", lambda e, o=ps[5][0:1, 0:512], a=M(ONESC),
                  r=A(src + pb * 512, 512), st=(pb == 0), sp=(pb == 7):
                  e.matmul(o, a, r, start=st, stop=sp), waits=w0)
        dn5 = now(P)
        w6a = [s_sq] + ([kb.bank_free[6]] if kb.bank_free[6] else [])
        for pb in range(8):
            kb.op("tensor", lambda e, o=ps[6][0:1, 0:512], a=M(ONESC),
                  r=A(R_SQ + pb * 512, 512), st=(pb == 0), sp=(pb == 7):
                  e.matmul(o, a, r, start=st, stop=sp),
                  waits=(w6a if pb == 0 else ()))
        dn6 = now(P)
        kb.vts(sMean, ps[5][0:1, 0:512], misc[0:1, CINV:CINV + 1], ALU.mult,
               waits=[dn5])
        kb.consume_bank(5, "vector")
        kb.vts(sMsq, ps[6][0:1, 0:512], misc[0:1, CINV:CINV + 1], ALU.mult,
               waits=[dn6])
        kb.consume_bank(6, "vector")
        kb.vtt(sTmp, sMean, sMean, ALU.mult)
        v_var = kb.vtt(sMsq, sMsq, sTmp, ALU.subtract)
        s_sqr = kb.sact(sTmp, sMsq, AF.Sqrt, bias=misc[0:1, CEPS:CEPS + 1],
                        waits=[v_var])
        v_inv = kb.vrecip(sMsq, sTmp, waits=[s_sqr])
        w6 = [v_inv] + ([kb.bank_free[6]] if kb.bank_free[6] else [])
        dn6b = kb.op("tensor", lambda e: e.matmul(ps[6][:, 0:512],
                     misc[0:1, ONESR:ONESR + 128], sMean,
                     start=True, stop=True), waits=w6)
        w7 = [v_inv] + ([kb.bank_free[7]] if kb.bank_free[7] else [])
        dn7 = kb.op("tensor", lambda e: e.matmul(ps[7][:, 0:512],
                    misc[0:1, ONESR:ONESR + 128], sMsq,
                    start=True, stop=True), waits=w7)
        v_p1 = None
        for pb in range(8):
            v_p1 = kb.vtt(A(R_TMP + pb * 512, 512), A(src + pb * 512, 512),
                          ps[6][:, 0:512], ALU.subtract,
                          waits=([dn6b] if pb == 0 else ()))
        kb.consume_bank(6, "vector")
        ret = None
        for pb in range(8):
            w = [dn7] if pb == 0 else []
            if pb == 0 and dst_guard is not None:
                w.extend(dst_guard if isinstance(dst_guard, list)
                         else [dst_guard])
            ret = kb.vtt(A(BO(dst, pb), 512), A(R_TMP + pb * 512, 512),
                         ps[7][:, 0:512], ALU.mult, waits=w)
        kb.consume_bank(7, "vector")
        return ret, v_p1

    def ffn_apply(xn, src, dst, v_xn, src_ready, dst_guard=None, stat_ready=()):
        evs = []
        dn = None
        for m in range(8):
            bank = m % 4
            mats = [(A(R_WEFF + k * 1024 + m * 128, 128), A(BO(xn, k), 512))
                    for k in range(8)]
            dn = kb.mm_group(bank, mats,
                             waits=[v_xn] + (list(stat_ready) if m == 0 else []))
            s_e = kb.vts2(A(R_TMP + m * 512, 512), ps[bank][:, 0:512], M(CHALF),
                          M(HALFB + m), ALU.mult, ALU.add, waits=[dn])
            w = [s_e]
            if m == 0:
                w.append(src_ready)
                if dst_guard is not None:
                    w.extend(dst_guard if isinstance(dst_guard, list)
                             else [dst_guard])
            ev = kb.vtt(A(dst + m * 512, 512), A(R_TMP + m * 512, 512),
                        A(src + m * 512, 512), ALU.add, waits=w)
            evs.append(ev)
            kb.consume_bank(bank, "vector")
        return evs[-1], evs, dn

    x_r = x_fm.rearrange("(k p) t -> k p t", p=128)
    x1_r = x1sp.ap().rearrange("(k p) t -> k p t", p=128)
    xnin_r = [x.ap().rearrange("(k p) t -> k p t", p=128) for x in xn_in]
    P0X = (R_XT, R_UWSTG)
    P0XN = (R_WIN,
            [R_DBC + i * 512 for i in range(4)] +
            [R_WTS + i * 512 for i in range(4)])
    h0 = {}
    for it in range(NTS + 1):
        ts, pv = it, it - 1
        if ts < NTS:
            p = ts % 2
            cs = slice(ts * TSZ, (ts + 1) * TSZ)
            w = ([h0[ts - 2]["ln2_p1"], h0[ts - 2]["l_x1dma"]]
                 if ts >= 2 else [])
            l_x = pre_lx.get(ts)
            if l_x is None:
                l_x = kb.dma(A(P0X[p], 4096),
                             (x_r[:, :, cs]).transpose([1, 0, 2]), waits=w)
            h0[ts] = {"l_x": l_x}
        if pv >= 0:
            p = pv % 2
            cs = slice(pv * TSZ, (pv + 1) * TSZ)
            v_x1, evs, pe_ffn = ffn_apply(P0XN[p], P0X[p], P0X[p],
                                          h0[pv]["v_xn1"], h0[pv]["l_x"])
            l_x1dma = kb.dma(x1_r[:, :, cs].transpose([1, 0, 2]),
                             A(P0X[p], 4096), waits=[v_x1])
            h0[pv].update(evs=evs, pe_ffn=pe_ffn, l_x1dma=l_x1dma)
        if ts < NTS:
            p = ts % 2
            dg = ([h0[ts - 2]["pe_ffn"]] + h0[ts - 2]["l_xnd"]
                  if ts >= 2 else None)
            v_xn1, _ = layer_norm(P0X[p], P0XN[p], h0[ts]["l_x"],
                                  dst_guard=dg)
            h0[ts]["v_xn1"] = v_xn1
        if pv >= 0:
            p = pv % 2
            v_xn2, ln2_p1 = layer_norm(P0X[p], P0XN[p], h0[pv]["evs"],
                                       dst_guard=[h0[pv]["pe_ffn"]])
            if p == 0:
                l_xnd = [kb.dma(xnin_r[pv].transpose([1, 0, 2]),
                                A(R_WIN, 4096), waits=[v_xn2])]
            else:
                l_xnd = [kb.dma(xnin_r[pv][0:4].transpose([1, 0, 2]),
                                A(R_DBC, 2048), waits=[v_xn2]),
                         kb.dma(xnin_r[pv][4:8].transpose([1, 0, 2]),
                                A(R_WTS, 2048), waits=[v_xn2])]
            h0[pv].update(ln2_p1=ln2_p1, l_xnd=l_xnd)
            kb.coll("AllGather", ALU.bypass, [xn_in[pv].ap().opt()],
                    [xn_out[pv].ap().opt()], waits=l_xnd)
    c_xn_ts = [(C, C.ctr - (NTS - 1) + t) for t in range(NTS)]

    xn_blk = [x.ap().rearrange("(r k p) t -> r k p t", k=8, p=128)
              for x in xn_out]
    rsin_r = [r.ap().rearrange("(blk k p) t -> blk k p t", k=8, p=128)
              for r in rs_in]
    c_rs4 = {}
    usp_r = u_sp.ap().rearrange("(s db p) t -> s db p t", db=2, p=128)
    zsp_r = z_sp.ap().rearrange("(s db p) t -> s db p t", db=2, p=128)
    NSL = 3
    fwd_block_written = {}
    v_yg_prev = None
    p_comp_prev = h0[3]["l_x1dma"]
    c_ar_map = {}
    acc_free = [None, None]
    s_halo = [None]
    stageslot_free = [None, None]
    stg_free = [[]]
    v_conv0 = {}
    dbc_dmas = {}
    p_win_prev = None
    p_delta_prev = None
    u_out_dma = {}
    z_out_dma = {}
    u_load_dma = {}
    z_load_dma = {}
    zstg_free = [h0[3]["ln2_p1"], h0[3]["ln2_p1"]]
    cstg_free = [None, None]
    last_scan_v = {}
    scan_state = {}
    v_w_last = [None]
    last_prod31 = [None]

    for d in ("f", "b"):
        rev = (d == "b")
        gi0 = 0 if d == "f" else NCH
        if d == "f":
            w_win = [h0[2]["pe_ffn"]] + h0[2]["l_xnd"]
            w_wts = list(l_wt) + [h0[3]["pe_ffn"]] + h0[3]["l_xnd"]
            w_rows = [h0[3]["ln2_p1"]]
            w_misc2 = []
            p_win_prev = [h0[2]["ln2_p1"], h0[2]["l_x1dma"]]
            conv0_extra = [h0[3]["pe_ffn"]]
        else:
            w_win = [p_win_prev]
            w_wts = list(l_wt) + [p_comp_prev]
            w_rows = [p_delta_prev] + list(stg_free[0])
            w_misc2 = [acc_free[1], v_w_last[0]]
            conv0_extra = []
        winT_r = winT[d].rearrange("(k p) m -> k p m", p=128)
        for k in range(8):
            kb.dma(A(R_WIN + k * 512, 512), winT_r[k],
                   waits=(w_win if k < 4 else ()))
        wts_r = wtsp[d].ap().rearrange("(m p) o -> m p o", p=128)
        for m in range(2):
            kb.dma(A(R_WTS + m * 1024, 1024), wts_r[m],
                   waits=(w_wts if m == 0 else ()))
        wx_r = wxT[d].rearrange("(c p) m -> c p m", p=128)
        kb.dma(misc[:, WX0:WX0 + 96], wx_r[0], waits=w_rows)
        kb.dma(misc[:, WX1:WX1 + 96], wx_r[1])
        kb.dma(misc[0:DTR, WDT:WDT + 256], wdtT[d])
        kb.dma(M(WTAP, 8), wconv[d], waits=w_misc2)
        kb.dma(M(BCONV, 2), bconv[d])
        kb.dma(M(BDT, 2), bdt[d])
        kb.dma(M(DPC, 2), dpv[d])
        kb.dma(misc[:, ALOG:ALOG + 32], alog[d])
        l_wd = kb.dma_all()
        s_a = kb.sact(M(A_SB, 32), misc[:, ALOG:ALOG + 32], AF.Exp, waits=l_wd)
        kb.vts(M(BDT, 2), M(BDT, 2), -1.0, ALU.mult, waits=l_wd)
        kb.vmemset(M(HCAR, 32), 0.0, waits=[s_a])
        kb.vmemset(arena[:, R_UIN:R_UIN + 4], 0.0)
        v_init = kb.vmemset(arena[:, R_UIN + 2052:R_UIN + 2056], 0.0)

        s_ev_cur = None

        def stageA_ts(c, ts):
            nonlocal p_win_prev, s_ev_cur
            c8 = c % NCH
            if not rev:
                srcx = xn_blk[ts][c8]
                c_xn = c_xn_ts[ts]
            else:
                srcx = xn_blk[NTS - 1 - ts][7 - c8]
                c_xn = c_xn_ts[NTS - 1 - ts]
            w = [c_xn]
            if p_win_prev:
                w.extend(p_win_prev if isinstance(p_win_prev, list)
                         else [p_win_prev])
            l_xt = kb.dma(A(R_XT, 4096), (srcx).transpose([1, 0, 2]), waits=w,
                          lane=2)
            slot = c % NSL
            for m in range(4):
                bank = m
                mov = [A(R_XT + k * 512, 512) for k in range(8)]
                if rev:
                    mov = [t[:, ::-1] for t in mov]
                mats = [(A(R_WIN + k * 512 + m * 128, 128), mov[k])
                        for k in range(8)]
                dn = kb.mm_group(bank, mats, waits=[l_xt])
                if m < 2:
                    kb.vcopy(arena[:, R_UIN + m * 2052 + 4 + ts * TSZ:
                                   R_UIN + m * 2052 + 4 + ts * TSZ + 512],
                             ps[bank][:, 0:512], waits=[dn])
                else:
                    db = m - 2
                    zslot = db
                    wv = [dn]
                    if zstg_free[zslot]:
                        wv.append(zstg_free[zslot])
                    s_z = kb.sact(arena[:, R_ZSTG + zslot * 512:R_ZSTG + zslot * 512 + 512],
                                  ps[bank][:, 0:512], AF.Silu, waits=wv)
                    zw = [s_z]
                    if (c - NSL) in z_load_dma:
                        zw += z_load_dma[c - NSL]
                    dma_t = kb.dma(zsp_r[slot][db][:, ts * TSZ:(ts + 1) * TSZ],
                                   arena[:, R_ZSTG + zslot * 512:R_ZSTG + zslot * 512 + 512],
                                   waits=zw, lane=2)
                    zstg_free[zslot] = dma_t
                    z_out_dma.setdefault(c, []).append(dma_t)
                kb.consume_bank(bank, "scalar" if m >= 2 else "vector")
            p_win_prev = now(P)
            s_ev_cur = now(V)
            slot_a = ts % 2
            for db in range(2):
                base = R_UIN + db * 2052 + 4 + ts * TSZ
                acc = arena[:, R_DBC + slot_a * 1024 + db * 512:
                            R_DBC + slot_a * 1024 + db * 512 + 512]
                wv = []
                if db == 0:
                    if acc_free[slot_a] is not None:
                        wv.append(acc_free[slot_a])
                    if stageslot_free[slot_a] is not None:
                        wv.append(stageslot_free[slot_a])
                    if ts == 0:
                        wv += [v_init]
                        if p_delta_prev:
                            wv.append(p_delta_prev)
                        if s_halo[0] is not None:
                            wv.append(s_halo[0])
                        if c8 == 0:
                            wv += conv0_extra
                kb.vts(acc, arena[:, base - 3:base - 3 + 512],
                       M(WTAP + db * 4), ALU.mult, waits=wv)
                for j in range(1, 4):
                    kb.vstt(acc, arena[:, base - 3 + j:base - 3 + j + 512],
                            M(WTAP + db * 4 + j), acc, ALU.mult, ALU.add)
            v_cv = now(V)
            if ts == 0:
                v_conv0[c] = v_cv
            s_silu = None
            for db in range(2):
                acc = arena[:, R_DBC + slot_a * 1024 + db * 512:
                            R_DBC + slot_a * 1024 + db * 512 + 512]
                sw = [v_cv] if db == 0 else []
                if db == 0:
                    sw += stg_free[0]
                s_silu = kb.sact(A(R_STG + db * 512, 512), acc, AF.Silu,
                                 bias=M(BCONV + db), waits=sw)
            acc_free[slot_a] = s_silu
            spl = []
            for db in range(2):
                uw = [s_silu]
                if (c - NSL) in u_load_dma:
                    uw += u_load_dma[c - NSL]
                dt_ = kb.dma(usp_r[slot][db][:, ts * TSZ:(ts + 1) * TSZ],
                             A(R_STG + db * 512, 512), waits=uw, lane=2)
                spl.append(dt_)
            u_out_dma.setdefault(c, []).extend(spl)
            mats = [(misc[:, (WX0 if kk == 0 else WX1):
                           (WX0 if kk == 0 else WX1) + 96],
                     A(R_STG + kk * 512, 512)) for kk in range(2)]
            dn4 = kb.mm_group(4, mats, waits=[s_silu] + list(l_wd),
                              out_rows=(0, 96))
            stg_free[0] = [dn4] + spl
            sd = R_DBC + ((ts + 1) % 2) * 1024
            s_dbc = kb.scopy(arena[0:96, sd:sd + 512], ps[4][0:96, 0:512],
                             waits=[dn4])
            kb.consume_bank(4, "scalar")
            l_blk = kb.dma(dbc_in[d][c8].ap()[:, ts * TSZ:(ts + 1) * TSZ],
                           arena[0:96, sd:sd + 512], waits=[s_dbc], lane=2)
            stageslot_free[(ts + 1) % 2] = l_blk
            dbc_dmas.setdefault(c, []).append(l_blk)

        def stageA_tail(c):
            for db in range(2):
                base = R_UIN + db * 2052
                s_halo[0] = kb.scopy(arena[:, base + 1:base + 4],
                                     arena[:, base + 2049:base + 2052],
                                     waits=([v_conv0[c]] if db == 0 else ()))
            c8 = c % NCH
            c_ar_map[c] = kb.coll("AllReduce", ALU.add,
                                  [dbc_in[d][c8].ap().opt()],
                                  [dbc_out[d][c8].ap().opt()],
                                  waits=dbc_dmas[c])

        def bcast_src(c, n, which):
            row = 64 + n if which == "B" else 80 + n
            return dbc_out[d][c % NCH].ap()[row:row + 1, :].unsqueeze(0) \
                .broadcast_to([128, 1, TOK])

        def scan_prologue(c):
            nonlocal p_delta_prev
            st = {"v_scan": {}, "g_row": {}, "v_prod": {}, "l_bc": {}}
            scan_state[c] = st
            c_ar = c_ar_map[c]
            wdt_ = [c_ar] + [e for e in acc_free + stageslot_free
                             if e is not None]
            l_dt = kb.dma(arena[0:DTR, R_DBC:R_DBC + 2048],
                          dbc_out[d][c % NCH].ap()[0:DTR, :], waits=wdt_,
                          lane=3)
            wb = [c_ar]
            if (c - 1) in last_scan_v:
                wb.append(last_scan_v[c - 1])
            b1 = kb.dma(A(R_BB, 2048).rearrange("p (g n) -> p g n", g=1),
                        bcast_src(c, 0, "B"), waits=wb, lane=0)
            b2 = kb.dma(A(R_CB, 2048).rearrange("p (g n) -> p g n", g=1),
                        bcast_src(c, 0, "C"), waits=wb, lane=1)
            st["l_bc"][0] = [b1, b2]
            uw = list(u_out_dma[c])
            if (c - 1) in last_scan_v:
                uw.append(last_scan_v[c - 1])
            lu = [kb.dma(A(R_UW + db * 2048, 2048), usp_r[c % NSL][db], waits=uw,
                         lane=3) for db in range(2)]

            u_load_dma[c] = lu
            for m in range(2):
                for ns in range(4):
                    bank = m
                    dn = kb.mm_group(bank,
                                     [(misc[0:DTR, WDT + m * 128:WDT + (m + 1) * 128],
                                       arena[0:DTR, R_DBC + ns * 512:R_DBC + (ns + 1) * 512])],
                                     waits=[l_dt])
                    wv = [dn]
                    if m == 0 and ns == 0 and (c - 1) in last_scan_v:
                        wv.append(last_scan_v[c - 1])
                    kb.sact(A(R_DA + ns * 512, 512), ps[bank][:, 0:512],
                            AF.Sigmoid, bias=M(BDT + m),
                            scale=M(CNEG1), waits=wv)
                    kb.consume_bank(bank, "scalar")
                s_dl = None
                for ns in range(4):
                    s_dl = kb.sact(A(R_DL + m * 2048 + ns * 512, 512),
                                   A(R_DA + ns * 512, 512), AF.Ln)
                st.setdefault("s_dl", {})[m] = s_dl
            st["lu"] = lu
            p_delta_prev = now(P)

        def scan_prologue_b(c):
            st = scan_state[c]
            st_prev = scan_state.get(c - 1)
            st["v_w_db"] = {}
            for db in range(2):
                wv = [st["lu"][db], st["s_dl"][db]]
                if db == 0 and st_prev is not None:
                    wv.append(st_prev["v_yg"])
                st.setdefault("v_init_db", {})[db] = kb.vts(
                    A(R_YACC + db * 2048, 2048), A(R_UW + db * 2048, 2048),
                    M(DPC + db), ALU.mult, waits=wv)
                st["v_w_db"][db] = kb.vstt(A(R_UW + db * 2048, 2048),
                                           A(R_DL + db * 2048, 2048),
                                           -1.0, A(R_UW + db * 2048, 2048),
                                           ALU.mult, ALU.mult)
            st["v_w"] = now(V)
            v_w_last[0] = st["v_w"]

        def scan_rows(c, n_list):
            st = scan_state[c]
            v_scan, g_row, v_prod, l_bc = (st["v_scan"], st["g_row"],
                                           st["v_prod"], st["l_bc"])
            for n in n_list:
                buf = n % 2
                if n + 1 < DS and (n + 1) not in l_bc:
                    nbuf = (n + 1) % 2
                    ww = []
                    if n == 0 and last_prod31[0] is not None:
                        ww.append(last_prod31[0])
                    if n >= 1:
                        rp = 2 * n - 1
                        ww = [g_row[rp]]
                        ww.append(v_prod[rp])
                    b1 = kb.dma(A(R_BB + nbuf * 2048, 2048)
                                .rearrange("p (g nn) -> p g nn", g=1),
                                bcast_src(c, n + 1, "B"), waits=ww, lane=0)
                    b2 = kb.dma(A(R_CB + nbuf * 2048, 2048)
                                .rearrange("p (g nn) -> p g nn", g=1),
                                bcast_src(c, n + 1, "C"), waits=ww, lane=1)
                    l_bc[n + 1] = [b1, b2]
                for db in range(2):
                    row = n * 2 + db
                    ch = db * 16 + n
                    wda = [st["v_w"]] if row < 1 else [v_scan[row - 1]]
                    s_da = kb.sact(A(R_DA, 2048), A(R_DL + db * 2048, 2048),
                                   AF.Exp, scale=M(A_SB + ch), waits=wda)
                    kb.vtt(A(R_DBU, 2048), A(R_UW + db * 2048, 2048),
                           A(R_BB + buf * 2048, 2048), ALU.mult,
                           waits=l_bc[n])
                    v_scan[row] = kb.vscan(A(R_H, 2048), A(R_DA, 2048),
                                           A(R_DBU, 2048), M(HCAR + ch),
                                           waits=[s_da])
                    kb.vcopy(M(HCAR + ch), arena[:, R_H + 2047:R_H + 2048])
                    v_prod[row] = kb.vtt(A(R_PROD, 2048), A(R_H, 2048),
                                         A(R_CB + buf * 2048, 2048), ALU.mult,
                                         waits=l_bc[n])
                    g_row[row] = kb.vtt(A(R_YACC + db * 2048, 2048),
                                        A(R_YACC + db * 2048, 2048),
                                        A(R_PROD, 2048), ALU.add)
            if (DS - 1) in n_list:
                last_scan_v[c] = v_scan[2 * DS - 1]
                last_prod31[0] = v_prod[2 * DS - 1]

        def gate_q(c):
            nonlocal v_yg_prev, p_comp_prev
            st = scan_state[c]
            g_row, v_scan = st["g_row"], st["v_scan"]
            slot = c % NSL
            v_yg = None
            for db in range(2):
                zw = list(z_out_dma[c]) + [v_scan[2 * DS - 1]]
                if db == 1:
                    zw.append(v_yg)
                lz = kb.dma(A(R_DBU, 2048), zsp_r[slot][db], waits=zw, lane=3)
                z_load_dma.setdefault(c, []).append(lz)
                wv = [lz, g_row[2 * (DS - 1) + db]]
                if db == 0 and p_comp_prev:
                    wv.append(p_comp_prev)
                v_yg = kb.vtt(A(R_UWSTG + db * 2048, 2048),
                              A(R_YACC + db * 2048, 2048),
                              A(R_DBU, 2048), ALU.mult, waits=wv)
            st["v_yg"] = v_yg
            v_yg_prev = v_yg

        def composite_tasks(c):
            st = scan_state[c]
            for ts2 in range(NTS):
                for m in range(8):
                    yield (c, ts2, m, st)

        comp_bank = [0]

        def emit_comp(task):
            nonlocal p_comp_prev
            c, ts2, m, st = task
            c8 = c % NCH
            bank = 5 + comp_bank[0] % 3
            comp_bank[0] += 1
            mats = [(A(R_WTS + kk * 1024 + m * 128, 128),
                     A(R_UWSTG + kk * 2048 + ts2 * TSZ, 512))
                    for kk in range(2)]
            dn = kb.mm_group(bank, mats, waits=[st["v_yg"]])
            cslot = (ts2 * 8 + m) % 2
            wv = [dn]
            if cstg_free[cslot]:
                wv.append(cstg_free[cslot])
            dst = arena[:, R_CSTG + cslot * 512:R_CSTG + cslot * 512 + 512]
            if rev:
                dst = dst[:, ::-1]
            s_pe = kb.scopy(dst, ps[bank][:, 0:512], waits=wv)
            kb.consume_bank(bank, "scalar")
            if not rev:
                sl = ts2
                dma_t = kb.dma(
                    rsin_r[sl][c8][m],
                    arena[:, R_CSTG + cslot * 512:R_CSTG + cslot * 512 + 512],
                    waits=[s_pe], lane=3)
            else:
                blk = 7 - c8
                sl = NTS - 1 - ts2
                dma_t = kb.dma(
                    rsin_r[sl][blk][m],
                    arena[:, R_CSTG + cslot * 512:R_CSTG + cslot * 512 + 512],
                    waits=[s_pe] + fwd_block_written[blk],
                    eng="gpsimd", accum=True)
                if c8 == NCH - 1 and m == 7:
                    c_rs4[sl] = kb.coll(
                        "ReduceScatter", ALU.add, [rs_in[sl].ap().opt()],
                        [rs_out[sl].ap().opt()], waits=[dma_t])
            cstg_free[cslot] = dma_t
            if ts2 == NTS - 1 and m == 7:
                if not rev:
                    fwd_block_written[c8] = kb.dma_all()
                p_comp_prev = now(P)
                del scan_state[c]

        ROW_SLOTS = {0: range(0, 2), 1: range(2, 5), 2: range(5, 8),
                     3: range(8, 11)}
        pending_comp = []

        def drain_comp(k):
            for _ in range(k):
                if pending_comp:
                    emit_comp(pending_comp.pop(0))

        def scan_rows_drip(c, n_list):
            for n in n_list:
                scan_rows(c, [n])
                drain_comp(2)

        for i in range(NCH + 2):
            sc = gi0 + i - 2
            if i >= 2:
                scan_prologue(sc)
                scan_prologue_b(sc)
            for ts in range(NTS):
                if i < NCH:
                    stageA_ts(gi0 + i, ts)
                if i >= 2:
                    scan_rows_drip(sc, range(3 * ts, 3 * ts + 3))
            if i < NCH:
                stageA_tail(gi0 + i)
            if i >= 2:
                scan_rows_drip(sc, range(12, DS))
                drain_comp(99)
                gate_q(sc)
                pending_comp = list(composite_tasks(sc))
        drain_comp(99)


    p3_bar = [now(S), now(V), now(P), now(G), (GD, GD.ctr)] + kb.dma_all()
    for k in range(8):
        kb.dma(A(R_WEFF + k * 1024, 1024), wsp_r[k],
               waits=(p3_bar if k < 4 else ()))
    l_weff2 = kb.dma_all()
    rs_r = [r.ap().rearrange("(k p) t -> k p t", p=128) for r in rs_out]
    out_r = out_t.rearrange("(k p) t -> k p t", p=128)
    P3X = (R_XT, R_CB)
    P3XN = (R_WIN, R_DA)
    P3RS = (R_UWSTG, R_H)
    h3 = {}
    TORD = list(range(NTS - 1, -1, -1))
    for it in range(NTS + 1):
        ts = TORD[it] if it < NTS else None
        pv = TORD[it - 1] if it >= 1 else None
        if ts is not None:
            p = it % 2
            cs = slice(ts * TSZ, (ts + 1) * TSZ)
            w = ([h3[TORD[it - 2]]["ln_p1"], h3[TORD[it - 2]]["l_outdma"]]
                 if it >= 2 else [])
            l_x1 = kb.dma(A(P3X[p], 4096),
                          (x1_r[:, :, cs]).transpose([1, 0, 2]), waits=w)
            wt = [c_rs4[ts]] + ([h3[TORD[it - 2]]["v_t"]] if it >= 2 else [])
            l_t = kb.dma(A(P3RS[p], 4096),
                         (rs_r[ts]).transpose([1, 0, 2]),
                         waits=wt)
            for pb in range(8):
                kb.vts(A(R_TMP + pb * 512, 512), A(P3RS[p] + pb * 512, 512),
                       M(BPROJ + pb), ALU.add, waits=([l_t] if pb == 0 else ()))
            v_t = now(V)
            evs_x2 = []
            for pb in range(8):
                ev = kb.vtt(A(P3X[p] + pb * 512, 512),
                            A(R_TMP + pb * 512, 512),
                            A(P3X[p] + pb * 512, 512), ALU.add,
                            waits=([l_x1] if pb == 0 else ()))
                evs_x2.append(ev)
            h3[ts] = {"l_x1": l_x1, "v_t": v_t, "evs_x2": evs_x2}
        if pv is not None:
            p = (it - 1) % 2
            cs = slice(pv * TSZ, (pv + 1) * TSZ)
            v_x3, _, pe_ffn = ffn_apply(P3XN[p], P3X[p], P3X[p],
                                        h3[pv]["v_xn3"], h3[pv]["evs_x2"][-1],
                                        stat_ready=l_weff2)
            l_outdma = kb.dma(out_r[:, :, cs].transpose([1, 0, 2]),
                              A(P3X[p], 4096), waits=[v_x3])
            h3[pv].update(pe_ffn=pe_ffn, l_outdma=l_outdma)
        if ts is not None:
            p = it % 2
            dg = [h3[TORD[it - 2]]["pe_ffn"]] if it >= 2 else None
            v_xn3, ln_p1 = layer_norm(P3X[p], P3XN[p], h3[ts]["evs_x2"],
                                      dst_guard=dg)
            h3[ts].update(v_xn3=v_xn3, ln_p1=ln_p1)
    final_waits = kb.dma_all()
    kb.op("sync", lambda e: [e.wait_ge(sm.h, v) for sm, v in final_waits],
          sem=kb.Ls[0], incn=0)

    return nc, kb


def _emit(nc, kb):
    import contextlib
    with contextlib.ExitStack() as st:
        for s in kb.sems:
            s.h = st.enter_context(nc.semaphore(s.name))
        blk = st.enter_context(nc.Block())
        blk.sync(lambda e: kb.emit(e, "sync"))
        blk.scalar(lambda e: kb.emit(e, "scalar"))
        blk.vector(lambda e: kb.emit(e, "vector"))
        blk.tensor(lambda e: kb.emit(e, "tensor"))
        blk.gpsimd(lambda e: kb.emit(e, "gpsimd"))
    return nc


def _host_inputs(c, a):
    t0, t1 = c * TOK, (c + 1) * TOK
    d0, d1 = c * DSL, (c + 1) * DSL
    ct = np.ascontiguousarray
    m = {}
    m["x_fm"] = ct(a["L0_attn_x"][0, t0:t1, :].T)
    for d, sfx in (("f", "_f"), ("b", "_b")):
        winT = a["L0_attn_win" + sfx].T
        m[f"winT_{d}"] = ct(np.concatenate(
            [winT[:, d0:d1], winT[:, DI + d0:DI + d1]], axis=1))
        m[f"wxT_{d}"] = ct(a["L0_attn_wx" + sfx].T[d0:d1, :])
        m[f"wdtT_{d}"] = ct(a["L0_attn_wdt" + sfx].T[:, d0:d1])
        m[f"wconv_{d}"] = ct(a["L0_attn_wconv" + sfx][d0:d1, 0, :]
                             .reshape(2, 128, 4).transpose(1, 0, 2).reshape(128, 8))
        m[f"bconv_{d}"] = ct(a["L0_attn_bconv" + sfx][d0:d1].reshape(2, 128).T)
        m[f"bdt_{d}"] = ct(a["L0_attn_bdt" + sfx][d0:d1].reshape(2, 128).T)
        m[f"alog_{d}"] = ct(a["L0_attn_Alog" + sfx][d0:d1]
                            .reshape(2, 128, DS).transpose(1, 0, 2).reshape(128, 32))
        m[f"dp_{d}"] = ct(a["L0_attn_Dp" + sfx][d0:d1].reshape(2, 128).T)
        m[f"woutc_{d}"] = ct(a["L0_attn_wout" + sfx][:, d0:d1])
        half = a["L0_attn_wproj"][:, :DM] if d == "f" else a["L0_attn_wproj"][:, DM:]
        m[f"wprojT_{d}"] = ct(half.T)
    m["w_T"] = ct(a["L0_ffn_W"].T)
    m["v_rT"] = ct(a["L0_ffn_V_r"].T)
    m["c_T"] = ct(a["L0_ffn_C"].T)
    m["b_ffn"] = ct(a["L0_ffn_b"].reshape(8, 128).T)
    m["bproj_v"] = ct(a["L0_attn_bproj"].reshape(8, 128).T)
    return m


def kernel(**inputs):
    a = {k: np.asarray(v, dtype=np.float32) for k, v in inputs.items()}
    nc, kb = build_program()
    _emit(nc, kb)
    in_maps = [_host_inputs(c, a) for c in range(NCORES)]
    res = run_bass_kernel_spmd(nc, in_maps, core_ids=list(range(NCORES)))
    outs = [np.asarray(res.results[c]["out"]).T for c in range(NCORES)]
    return np.concatenate(outs, axis=0).reshape(1, SEQ, DM)


# version 21
# speedup vs baseline: 1.0438x; 1.0003x over previous
import numpy as np
import concourse.bass as bass
import concourse.mybir as mybir
from concourse.bass_utils import run_bass_kernel_spmd

F32 = mybir.dt.float32
AF = mybir.ActivationFunctionType
ALU = mybir.AluOpType

NCORES = 8
DM = 1024
DI = 2048
DS = 16
DTR = 64
SEQ = 16384
TOK = SEQ // NCORES
DSL = DI // NCORES
NTS = 4
TSZ = TOK // NTS
NCH = 8
EPS = 1e-5


class Sem:
    def __init__(self, name):
        self.name, self.h, self.ctr = name, None, 0


class KB:
    ENGS = ("sync", "scalar", "vector", "tensor", "gpsimd")

    def __init__(self, nc):
        self.nc = nc
        self.streams = {e: [] for e in self.ENGS}
        self.Ls = [Sem(f"L{i}") for i in range(4)]
        self.lrot = 0
        self.S = Sem("S")
        self.V = Sem("V")
        self.P = Sem("P")
        self.G = Sem("G")
        self.GD = Sem("GD")
        self.GA = Sem("GA")
        self.C = Sem("C")
        self.sems = self.Ls + [self.S, self.V, self.P, self.G, self.GD, self.GA,
                               self.C]
        self.esem = {"scalar": self.S, "vector": self.V,
                     "tensor": self.P, "gpsimd": self.G}
        self.bank_free = [None] * 8

    def dma_all(self):
        return [(s, s.ctr) for s in self.Ls if s.ctr]

    def op(self, eng, fn, waits=(), sem=None, incn=1):
        s = sem if sem is not None else self.esem[eng]
        waits = list(waits)
        if eng in ("scalar", "vector", "gpsimd") and s is self.esem.get(eng) and s.ctr:
            waits.append((s, s.ctr))
        s.ctr += incn
        waits = tuple((sm, v) for (sm, v) in waits if isinstance(sm, Sem) and v > 0)
        self.streams[eng].append((fn, waits, s, incn))
        return (s, s.ctr)

    def dma(self, out, in_, waits=(), eng="sync", accum=False, lane=None,
            gsem=None):
        if eng == "gpsimd":
            s = gsem if gsem is not None else self.GD
        else:
            if lane is None:
                s = self.Ls[self.lrot]
                self.lrot = (self.lrot + 1) % len(self.Ls)
            else:
                s = self.Ls[lane]
        if s.ctr:
            waits = list(waits) + [(s, s.ctr)]
        if accum:
            return self.op(eng, lambda e, o=out, i=in_:
                           e.dma_start(out=o, in_=i, accum_op=ALU.add),
                           waits=waits, sem=s, incn=16)
        return self.op(eng, lambda e, o=out, i=in_: e.dma_start(out=o, in_=i),
                       waits=waits, sem=s, incn=16)

    def mm_group(self, bank, mats, ncols=512, waits=(), out_rows=None):
        w = list(waits)
        if self.bank_free[bank] is not None:
            w.append(self.bank_free[bank])
        n = len(mats)
        last = None
        for j, (lhsT, rhs) in enumerate(mats):
            o = self.ps[bank][:, 0:ncols] if out_rows is None else \
                self.ps[bank][out_rows[0]:out_rows[1], 0:ncols]
            last = self.op("tensor",
                           lambda e, oo=o, a=lhsT, r=rhs, st=(j == 0), sp=(j == n - 1):
                           e.matmul(oo, a, r, start=st, stop=sp),
                           waits=(w if j == 0 else ()))
        return last

    def consume_bank(self, bank, eng):
        self.bank_free[bank] = (self.esem[eng], self.esem[eng].ctr)

    def sact(self, out, in_, func, bias=0.0, scale=1.0, waits=()):
        return self.op("scalar", lambda e, o=out, i=in_, f=func, bb=bias, ss=scale:
                       e.activation(o, i, f, bias=bb, scale=ss), waits=waits)

    def scopy(self, out, in_, waits=()):
        return self.op("scalar", lambda e, o=out, i=in_: e.copy(o, i), waits=waits)

    def vtt(self, out, i0, i1, opk, waits=()):
        return self.op("vector", lambda e, o=out, a=i0, c=i1, k=opk:
                       e.tensor_tensor(o, a, c, k), waits=waits)

    def vstt(self, out, i0, scal, i1, op0, op1, waits=()):
        return self.op("vector", lambda e, o=out, a=i0, s=scal, c=i1, k0=op0, k1=op1:
                       e.scalar_tensor_tensor(o, a, s, c, k0, k1), waits=waits)

    def vts(self, out, i0, scal, opk, waits=()):
        return self.op("vector", lambda e, o=out, a=i0, s=scal, k=opk:
                       e.tensor_single_scalar(o, a, s, k), waits=waits)

    def vts2(self, out, i0, s1, s2, op0, op1, waits=()):
        return self.op("vector", lambda e, o=out, a=i0, x=s1, y=s2, k0=op0, k1=op1:
                       e.tensor_scalar(o, a, x, y, k0, k1), waits=waits)

    def vcopy(self, out, in_, waits=()):
        return self.op("vector", lambda e, o=out, i=in_: e.tensor_copy(o, i),
                       waits=waits)

    def gtt(self, out, i0, i1, opk, waits=()):
        return self.op("gpsimd", lambda e, o=out, a=i0, c=i1, k=opk:
                       e.tensor_tensor(o, a, c, k), waits=waits)

    def vscan(self, out, d0, d1, init, waits=()):
        return self.op("vector", lambda e, o=out, a=d0, c=d1, ii=init:
                       e.tensor_tensor_scan(o, a, c, ii, ALU.mult, ALU.add),
                       waits=waits)

    def vmemset(self, ap, val, waits=()):
        return self.op("vector", lambda e, a=ap, v=val: e.memset(a, v), waits=waits)

    def vrecip(self, out, in_, waits=()):
        return self.op("vector", lambda e, o=out, i=in_: e.reciprocal(o, i),
                       waits=waits)

    def coll(self, kind, opk, ins, outs, waits=()):
        return self.op("gpsimd", lambda e, k=kind, o=opk, i=ins, u=outs:
                       e.collective_compute(k, o, replica_groups=[list(range(NCORES))],
                                            ins=i, outs=u),
                       waits=waits, sem=self.C)

    def emit(self, eng_obj, eng_name):
        for fn, waits, sem, incn in self.streams[eng_name]:
            for (sm, val) in waits:
                eng_obj.wait_ge(sm.h, val)
            ins = fn(eng_obj)
            if incn:
                ins.then_inc(sem.h, incn)


def build_program():
    nc = bass.Bass("TRN2", target_bir_lowering=False)

    def inp(name, shape):
        return nc.dram_tensor(name, list(shape), F32, kind="ExternalInput").ap()

    x_fm = inp("x_fm", (DM, TOK))
    winT = {d: inp(f"winT_{d}", (DM, 2 * DSL)) for d in "fb"}
    wxT = {d: inp(f"wxT_{d}", (DSL, 96)) for d in "fb"}
    wdtT = {d: inp(f"wdtT_{d}", (DTR, DSL)) for d in "fb"}
    wconv = {d: inp(f"wconv_{d}", (128, 8)) for d in "fb"}
    bconv = {d: inp(f"bconv_{d}", (128, 2)) for d in "fb"}
    bdt = {d: inp(f"bdt_{d}", (128, 2)) for d in "fb"}
    alog = {d: inp(f"alog_{d}", (128, 32)) for d in "fb"}
    dpv = {d: inp(f"dp_{d}", (128, 2)) for d in "fb"}
    woutc = {d: inp(f"woutc_{d}", (DM, DSL)) for d in "fb"}
    wprojT = {d: inp(f"wprojT_{d}", (DM, DM)) for d in "fb"}
    w_T = inp("w_T", (DM, DM))
    v_rT = inp("v_rT", (32, DM))
    c_T = inp("c_T", (32, DM))
    b_ffn = inp("b_ffn", (128, 8))
    bproj_v = inp("bproj_v", (128, 8))

    out_t = nc.dram_tensor("out", [DM, TOK], F32, kind="ExternalOutput").ap()

    def idram(name, shape):
        return nc.dram_tensor(name, list(shape), F32)

    x1sp = idram("x1sp", (DM, TOK))
    weffsp = idram("weffsp", (DM, DM))
    wtsp = {d: idram(f"wtsp_{d}", (2 * 128, DM)) for d in "fb"}
    xn_in = [idram(f"xn_in{t}", (DM, TSZ)) for t in range(NTS)]
    xn_out = [nc.dram_tensor(f"xn_out{t}", [NCORES * DM, TSZ], F32,
                             addr_space="Shared") for t in range(NTS)]
    dbc_in = {d: [idram(f"dbcin_{d}{c}", (96, TOK)) for c in range(NCH)] for d in "fb"}
    dbc_out = {d: [idram(f"dbcout_{d}{c}", (96, TOK)) for c in range(NCH)]
               for d in "fb"}
    rs_in = [idram(f"rs_in{h}", (NCORES * DM, TSZ)) for h in range(NTS)]
    rs_out = [idram(f"rs_out{h}", (DM, TSZ)) for h in range(NTS)]
    u_sp = idram("u_sp", (3 * 2 * 128, TOK))
    z_sp = idram("z_sp", (3 * 2 * 128, TOK))

    ARENA = 52232
    arena = nc.alloc_sbuf_tensor("arena", [128, ARENA], F32).ap()
    misc = nc.alloc_sbuf_tensor("misc", [128, 800], F32).ap()

    A_SB, WTAP, BCONV, BDT, DPC = 0, 32, 40, 42, 44
    BFFN, HALFB, BPROJ, ONESC, HCAR = 46, 54, 62, 70, 72
    ROWS, ONESR = 104, 640
    CEPS, CINV, CHALF, CNEG1 = 768, 769, 770, 771
    WX0, WX1, WDT, ALOG = ROWS, ROWS + 96, ROWS + 192, ROWS + 448

    def M(o, n=1):
        return misc[:, o:o + n]

    off = [0]

    def alloc(n):
        a = off[0]
        off[0] += n
        assert off[0] <= ARENA, f"arena overflow {off[0]}"
        return a

    R_XT = alloc(4096)
    R_WIN = alloc(4096)
    R_UIN = alloc(4104)
    R_UWSTG = alloc(4096)
    R_ZSTG = alloc(1024)
    R_DBC = alloc(2048)
    R_DL = alloc(4096)
    R_UW = alloc(4096)
    R_BB = alloc(4096)
    R_CB = alloc(4096)
    R_DA = alloc(2048)
    R_DBU = alloc(2048)
    R_H = alloc(2048)
    R_PROD = alloc(2048)
    R_YACC = alloc(4096)
    R_WTS = alloc(2048)
    R_CSTG = alloc(1024)
    R_STG = alloc(1024)
    P1_END = off[0]
    R_X, R_XN, R_TMP, R_SQ = R_XT, R_WIN, R_UIN, R_DL
    R_WEFF = R_UW
    R_WOC = R_CB
    R_WPT = R_YACC
    R_WPT = R_DA
    R_RS = R_UWSTG
    R_CMP = R_YACC
    R_SCR = R_ZSTG
    def A(o, n):
        return arena[:, o:o + n]

    ps = [nc.alloc_psum_tensor(f"ps{i}", [128, 512], F32).ap() for i in range(8)]
    kb = KB(nc)
    kb.ps = ps
    S, V, P, G, GD, C = kb.S, kb.V, kb.P, kb.G, kb.GD, kb.C

    def now(sem):
        return (sem, sem.ctr)

    kb.vmemset(M(ONESC), 1.0)
    kb.vmemset(misc[0:1, ONESR:ONESR + 128], 1.0)
    kb.vmemset(M(CEPS), EPS)
    kb.vmemset(M(CINV), 1.0 / DM)
    kb.vmemset(M(CHALF), 0.5)
    kb.vmemset(M(CNEG1), -1.0)
    l_b1 = kb.dma(M(BFFN, 8), b_ffn)
    l_b2 = kb.dma(M(BPROJ, 8), bproj_v)
    kb.vts(M(HALFB, 8), M(BFFN, 8), 0.5, ALU.mult, waits=[l_b1])

    x_r0 = x_fm.rearrange("(k p) t -> k p t", p=128)
    pre_lx = {}
    for ts0 in range(2):
        dst0 = (R_XT, R_UWSTG)[ts0]
        pre_lx[ts0] = kb.dma(A(dst0, 4096),
                             (x_r0[:, :, ts0 * TSZ:(ts0 + 1) * TSZ])
                             .transpose([1, 0, 2]))
    wT_r = w_T.rearrange("(k p) o -> k p o", p=128)
    l_wT = None
    for k in range(8):
        l_wT = kb.dma(A(R_WEFF + k * 1024, 1024), wT_r[k])
    l_v1 = kb.dma(arena[0:32, R_TMP:R_TMP + DM], v_rT)
    l_v2 = kb.dma(arena[0:32, R_TMP + DM:R_TMP + 2 * DM], c_T)
    w_all = kb.dma_all()
    for k in range(8):
        for hf in range(2):
            bank = hf
            dn = kb.mm_group(bank,
                             [(arena[0:32, R_TMP + k * 128:R_TMP + (k + 1) * 128],
                               arena[0:32, R_TMP + DM + hf * 512:R_TMP + DM + (hf + 1) * 512])],
                             waits=w_all)
            dst = A(R_WEFF + k * 1024 + hf * 512, 512)
            kb.vtt(dst, dst, ps[bank][:, 0:512], ALU.add, waits=[dn])
            kb.consume_bank(bank, "vector")
    v_weff = now(V)
    wsp_r = weffsp.ap().rearrange("(k p) o -> k p o", p=128)
    l_weffsp = None
    for k in range(8):
        l_weffsp = kb.dma(wsp_r[k], A(R_WEFF + k * 1024, 1024), waits=[v_weff])

    for d in "fb":
        wc_r = woutc[d].rearrange("(k p) i -> k p i", p=128)
        wp_r = wprojT[d].rearrange("(k p) o -> k p o", p=128)
        for k in range(8):
            kb.dma(A(R_WOC + k * 256, 256), wc_r[k])
            kb.dma(A(R_WPT + k * 1024, 1024), wp_r[k])
        w_all = kb.dma_all()
        for m in range(2):
            for hf in range(2):
                bank = 2 + hf
                mats = [(A(R_WOC + k * 256 + m * 128, 128),
                         A(R_WPT + k * 1024 + hf * 512, 512)) for k in range(8)]
                dn = kb.mm_group(bank, mats, waits=w_all)
                kb.scopy(A(R_CMP + m * 1024 + hf * 512, 512), ps[bank][:, 0:512],
                         waits=[dn])
                kb.consume_bank(bank, "scalar")
        s_cmp = now(S)
        wt_r = wtsp[d].ap().rearrange("(m p) o -> m p o", p=128)
        for m in range(2):
            kb.dma(wt_r[m], A(R_CMP + m * 1024, 1024), waits=[s_cmp])
    l_wt = kb.dma_all()

    def BO(b, i):
        return b + i * 512 if isinstance(b, int) else b[i]

    def layer_norm(src, dst, ready, dst_guard=None, sq_guard=None):
        rlist = ready if isinstance(ready, list) else None
        sMean = arena[0:1, R_SCR:R_SCR + 512]
        sMsq = arena[0:1, R_SCR + 512:R_SCR + 1024]
        sTmp = misc[0:1, ROWS:ROWS + 512]
        for pb in range(8):
            w = []
            if rlist is not None:
                w.append(rlist[pb])
            elif pb == 0:
                w.append(ready)
            if pb == 0 and sq_guard is not None:
                w.append(sq_guard)
            kb.sact(A(R_SQ + pb * 512, 512), A(src + pb * 512, 512), AF.Square,
                    waits=w)
        s_sq = now(S)
        for pb in range(8):
            w0 = []
            if rlist is not None:
                w0.append(rlist[pb])
            elif pb == 0:
                w0.append(ready)
            if pb == 0 and kb.bank_free[5]:
                w0.append(kb.bank_free[5])
            kb.op("tensor", lambda e, o=ps[5][0:1, 0:512], a=M(ONESC),
                  r=A(src + pb * 512, 512), st=(pb == 0), sp=(pb == 7):
                  e.matmul(o, a, r, start=st, stop=sp), waits=w0)
        dn5 = now(P)
        w6a = [s_sq] + ([kb.bank_free[6]] if kb.bank_free[6] else [])
        for pb in range(8):
            kb.op("tensor", lambda e, o=ps[6][0:1, 0:512], a=M(ONESC),
                  r=A(R_SQ + pb * 512, 512), st=(pb == 0), sp=(pb == 7):
                  e.matmul(o, a, r, start=st, stop=sp),
                  waits=(w6a if pb == 0 else ()))
        dn6 = now(P)
        kb.vts(sMean, ps[5][0:1, 0:512], misc[0:1, CINV:CINV + 1], ALU.mult,
               waits=[dn5])
        kb.consume_bank(5, "vector")
        kb.vts(sMsq, ps[6][0:1, 0:512], misc[0:1, CINV:CINV + 1], ALU.mult,
               waits=[dn6])
        kb.consume_bank(6, "vector")
        kb.vtt(sTmp, sMean, sMean, ALU.mult)
        v_var = kb.vtt(sMsq, sMsq, sTmp, ALU.subtract)
        s_sqr = kb.sact(sTmp, sMsq, AF.Sqrt, bias=misc[0:1, CEPS:CEPS + 1],
                        waits=[v_var])
        v_inv = kb.vrecip(sMsq, sTmp, waits=[s_sqr])
        w6 = [v_inv] + ([kb.bank_free[6]] if kb.bank_free[6] else [])
        dn6b = kb.op("tensor", lambda e: e.matmul(ps[6][:, 0:512],
                     misc[0:1, ONESR:ONESR + 128], sMean,
                     start=True, stop=True), waits=w6)
        w7 = [v_inv] + ([kb.bank_free[7]] if kb.bank_free[7] else [])
        dn7 = kb.op("tensor", lambda e: e.matmul(ps[7][:, 0:512],
                    misc[0:1, ONESR:ONESR + 128], sMsq,
                    start=True, stop=True), waits=w7)
        v_p1 = None
        for pb in range(8):
            v_p1 = kb.vtt(A(R_TMP + pb * 512, 512), A(src + pb * 512, 512),
                          ps[6][:, 0:512], ALU.subtract,
                          waits=([dn6b] if pb == 0 else ()))
        kb.consume_bank(6, "vector")
        ret = None
        for pb in range(8):
            w = [dn7] if pb == 0 else []
            if pb == 0 and dst_guard is not None:
                w.extend(dst_guard if isinstance(dst_guard, list)
                         else [dst_guard])
            ret = kb.vtt(A(BO(dst, pb), 512), A(R_TMP + pb * 512, 512),
                         ps[7][:, 0:512], ALU.mult, waits=w)
        kb.consume_bank(7, "vector")
        return ret, v_p1

    def ffn_apply(xn, src, dst, v_xn, src_ready, dst_guard=None, stat_ready=()):
        evs = []
        dn = None
        for m in range(8):
            bank = m % 4
            mats = [(A(R_WEFF + k * 1024 + m * 128, 128), A(BO(xn, k), 512))
                    for k in range(8)]
            dn = kb.mm_group(bank, mats,
                             waits=[v_xn] + (list(stat_ready) if m == 0 else []))
            s_e = kb.vts2(A(R_TMP + m * 512, 512), ps[bank][:, 0:512], M(CHALF),
                          M(HALFB + m), ALU.mult, ALU.add, waits=[dn])
            w = [s_e]
            if m == 0:
                w.append(src_ready)
                if dst_guard is not None:
                    w.extend(dst_guard if isinstance(dst_guard, list)
                             else [dst_guard])
            ev = kb.vtt(A(dst + m * 512, 512), A(R_TMP + m * 512, 512),
                        A(src + m * 512, 512), ALU.add, waits=w)
            evs.append(ev)
            kb.consume_bank(bank, "vector")
        return evs[-1], evs, dn

    x_r = x_fm.rearrange("(k p) t -> k p t", p=128)
    x1_r = x1sp.ap().rearrange("(k p) t -> k p t", p=128)
    xnin_r = [x.ap().rearrange("(k p) t -> k p t", p=128) for x in xn_in]
    P0X = (R_XT, R_UWSTG)
    P0XN = (R_WIN,
            [R_DBC + i * 512 for i in range(4)] +
            [R_WTS + i * 512 for i in range(4)])
    h0 = {}
    for it in range(NTS + 1):
        ts, pv = it, it - 1
        if ts < NTS:
            p = ts % 2
            cs = slice(ts * TSZ, (ts + 1) * TSZ)
            w = ([h0[ts - 2]["ln2_p1"], h0[ts - 2]["l_x1dma"]]
                 if ts >= 2 else [])
            l_x = pre_lx.get(ts)
            if l_x is None:
                l_x = kb.dma(A(P0X[p], 4096),
                             (x_r[:, :, cs]).transpose([1, 0, 2]), waits=w)
            h0[ts] = {"l_x": l_x}
        if pv >= 0:
            p = pv % 2
            cs = slice(pv * TSZ, (pv + 1) * TSZ)
            v_x1, evs, pe_ffn = ffn_apply(P0XN[p], P0X[p], P0X[p],
                                          h0[pv]["v_xn1"], h0[pv]["l_x"])
            l_x1dma = kb.dma(x1_r[:, :, cs].transpose([1, 0, 2]),
                             A(P0X[p], 4096), waits=[v_x1])
            h0[pv].update(evs=evs, pe_ffn=pe_ffn, l_x1dma=l_x1dma)
        if ts < NTS:
            p = ts % 2
            dg = ([h0[ts - 2]["pe_ffn"]] + h0[ts - 2]["l_xnd"]
                  if ts >= 2 else None)
            v_xn1, _ = layer_norm(P0X[p], P0XN[p], h0[ts]["l_x"],
                                  dst_guard=dg)
            h0[ts]["v_xn1"] = v_xn1
        if pv >= 0:
            p = pv % 2
            v_xn2, ln2_p1 = layer_norm(P0X[p], P0XN[p], h0[pv]["evs"],
                                       dst_guard=[h0[pv]["pe_ffn"]])
            if p == 0:
                l_xnd = [kb.dma(xnin_r[pv].transpose([1, 0, 2]),
                                A(R_WIN, 4096), waits=[v_xn2])]
            else:
                l_xnd = [kb.dma(xnin_r[pv][0:4].transpose([1, 0, 2]),
                                A(R_DBC, 2048), waits=[v_xn2]),
                         kb.dma(xnin_r[pv][4:8].transpose([1, 0, 2]),
                                A(R_WTS, 2048), waits=[v_xn2])]
            h0[pv].update(ln2_p1=ln2_p1, l_xnd=l_xnd)
            kb.coll("AllGather", ALU.bypass, [xn_in[pv].ap().opt()],
                    [xn_out[pv].ap().opt()], waits=l_xnd)
    c_xn_ts = [(C, C.ctr - (NTS - 1) + t) for t in range(NTS)]

    xn_blk = [x.ap().rearrange("(r k p) t -> r k p t", k=8, p=128)
              for x in xn_out]
    rsin_r = [r.ap().rearrange("(blk k p) t -> blk k p t", k=8, p=128)
              for r in rs_in]
    c_rs4 = {}
    usp_r = u_sp.ap().rearrange("(s db p) t -> s db p t", db=2, p=128)
    zsp_r = z_sp.ap().rearrange("(s db p) t -> s db p t", db=2, p=128)
    NSL = 3
    fwd_block_written = {}
    v_yg_prev = None
    p_comp_prev = h0[3]["l_x1dma"]
    c_ar_map = {}
    acc_free = [None, None]
    s_halo = [None]
    stageslot_free = [None, None]
    stg_free = [[]]
    v_conv0 = {}
    dbc_dmas = {}
    p_win_prev = None
    p_delta_prev = None
    u_out_dma = {}
    z_out_dma = {}
    u_load_dma = {}
    z_load_dma = {}
    zstg_free = [h0[3]["ln2_p1"], h0[3]["ln2_p1"]]
    cstg_free = [None, None]
    last_scan_v = {}
    scan_state = {}
    v_w_last = [None]
    last_prod31 = [None]

    for d in ("f", "b"):
        rev = (d == "b")
        gi0 = 0 if d == "f" else NCH
        if d == "f":
            w_win = [h0[2]["pe_ffn"]] + h0[2]["l_xnd"]
            w_wts = list(l_wt) + [h0[3]["pe_ffn"]] + h0[3]["l_xnd"]
            w_rows = [h0[3]["ln2_p1"]]
            w_misc2 = []
            p_win_prev = [h0[2]["ln2_p1"], h0[2]["l_x1dma"]]
            conv0_extra = [h0[3]["pe_ffn"]]
        else:
            w_win = [p_win_prev]
            w_wts = list(l_wt) + [p_comp_prev]
            w_rows = [p_delta_prev] + list(stg_free[0])
            w_misc2 = [acc_free[1], v_w_last[0]]
            conv0_extra = []
        winT_r = winT[d].rearrange("(k p) m -> k p m", p=128)
        for k in range(8):
            kb.dma(A(R_WIN + k * 512, 512), winT_r[k],
                   waits=(w_win if k < 4 else ()))
        wts_r = wtsp[d].ap().rearrange("(m p) o -> m p o", p=128)
        for m in range(2):
            kb.dma(A(R_WTS + m * 1024, 1024), wts_r[m],
                   waits=(w_wts if m == 0 else ()))
        wx_r = wxT[d].rearrange("(c p) m -> c p m", p=128)
        kb.dma(misc[:, WX0:WX0 + 96], wx_r[0], waits=w_rows)
        kb.dma(misc[:, WX1:WX1 + 96], wx_r[1])
        kb.dma(misc[0:DTR, WDT:WDT + 256], wdtT[d])
        kb.dma(M(WTAP, 8), wconv[d], waits=w_misc2)
        kb.dma(M(BCONV, 2), bconv[d])
        kb.dma(M(BDT, 2), bdt[d])
        kb.dma(M(DPC, 2), dpv[d])
        kb.dma(misc[:, ALOG:ALOG + 32], alog[d])
        l_wd = kb.dma_all()
        s_a = kb.sact(M(A_SB, 32), misc[:, ALOG:ALOG + 32], AF.Exp, waits=l_wd)
        kb.vts(M(BDT, 2), M(BDT, 2), -1.0, ALU.mult, waits=l_wd)
        kb.vmemset(M(HCAR, 32), 0.0, waits=[s_a])
        kb.vmemset(arena[:, R_UIN:R_UIN + 4], 0.0)
        v_init = kb.vmemset(arena[:, R_UIN + 2052:R_UIN + 2056], 0.0)

        s_ev_cur = None

        def stageA_ts(c, ts):
            nonlocal p_win_prev, s_ev_cur
            c8 = c % NCH
            if not rev:
                srcx = xn_blk[ts][c8]
                c_xn = c_xn_ts[ts]
            else:
                srcx = xn_blk[NTS - 1 - ts][7 - c8]
                c_xn = c_xn_ts[NTS - 1 - ts]
            w = [c_xn]
            if p_win_prev:
                w.extend(p_win_prev if isinstance(p_win_prev, list)
                         else [p_win_prev])
            l_xt = kb.dma(A(R_XT, 4096), (srcx).transpose([1, 0, 2]), waits=w,
                          lane=2)
            slot = c % NSL
            for m in range(4):
                bank = m
                mov = [A(R_XT + k * 512, 512) for k in range(8)]
                if rev:
                    mov = [t[:, ::-1] for t in mov]
                mats = [(A(R_WIN + k * 512 + m * 128, 128), mov[k])
                        for k in range(8)]
                dn = kb.mm_group(bank, mats, waits=[l_xt])
                if m < 2:
                    kb.vcopy(arena[:, R_UIN + m * 2052 + 4 + ts * TSZ:
                                   R_UIN + m * 2052 + 4 + ts * TSZ + 512],
                             ps[bank][:, 0:512], waits=[dn])
                else:
                    db = m - 2
                    zslot = db
                    wv = [dn]
                    if zstg_free[zslot]:
                        wv.append(zstg_free[zslot])
                    s_z = kb.sact(arena[:, R_ZSTG + zslot * 512:R_ZSTG + zslot * 512 + 512],
                                  ps[bank][:, 0:512], AF.Silu, waits=wv)
                    zw = [s_z]
                    if (c - NSL) in z_load_dma:
                        zw += z_load_dma[c - NSL]
                    dma_t = kb.dma(zsp_r[slot][db][:, ts * TSZ:(ts + 1) * TSZ],
                                   arena[:, R_ZSTG + zslot * 512:R_ZSTG + zslot * 512 + 512],
                                   waits=zw, lane=2)
                    zstg_free[zslot] = dma_t
                    z_out_dma.setdefault(c, []).append(dma_t)
                kb.consume_bank(bank, "scalar" if m >= 2 else "vector")
            p_win_prev = now(P)
            s_ev_cur = now(V)
            slot_a = ts % 2
            for db in range(2):
                base = R_UIN + db * 2052 + 4 + ts * TSZ
                acc = arena[:, R_DBC + slot_a * 1024 + db * 512:
                            R_DBC + slot_a * 1024 + db * 512 + 512]
                wv = []
                if db == 0:
                    if acc_free[slot_a] is not None:
                        wv.append(acc_free[slot_a])
                    if stageslot_free[slot_a] is not None:
                        wv.append(stageslot_free[slot_a])
                    if ts == 0:
                        wv += [v_init]
                        if p_delta_prev:
                            wv.append(p_delta_prev)
                        if s_halo[0] is not None:
                            wv.append(s_halo[0])
                        if c8 == 0:
                            wv += conv0_extra
                kb.vts(acc, arena[:, base - 3:base - 3 + 512],
                       M(WTAP + db * 4), ALU.mult, waits=wv)
                for j in range(1, 4):
                    kb.vstt(acc, arena[:, base - 3 + j:base - 3 + j + 512],
                            M(WTAP + db * 4 + j), acc, ALU.mult, ALU.add)
            v_cv = now(V)
            if ts == 0:
                v_conv0[c] = v_cv
            s_silu = None
            for db in range(2):
                acc = arena[:, R_DBC + slot_a * 1024 + db * 512:
                            R_DBC + slot_a * 1024 + db * 512 + 512]
                sw = [v_cv] if db == 0 else []
                if db == 0:
                    sw += stg_free[0]
                s_silu = kb.sact(A(R_STG + db * 512, 512), acc, AF.Silu,
                                 bias=M(BCONV + db), waits=sw)
            acc_free[slot_a] = s_silu
            spl = []
            for db in range(2):
                uw = [s_silu]
                if (c - NSL) in u_load_dma:
                    uw += u_load_dma[c - NSL]
                dt_ = kb.dma(usp_r[slot][db][:, ts * TSZ:(ts + 1) * TSZ],
                             A(R_STG + db * 512, 512), waits=uw, lane=2)
                spl.append(dt_)
            u_out_dma.setdefault(c, []).extend(spl)
            mats = [(misc[:, (WX0 if kk == 0 else WX1):
                           (WX0 if kk == 0 else WX1) + 96],
                     A(R_STG + kk * 512, 512)) for kk in range(2)]
            dn4 = kb.mm_group(4, mats, waits=[s_silu] + list(l_wd),
                              out_rows=(0, 96))
            stg_free[0] = [dn4] + spl
            sd = R_DBC + ((ts + 1) % 2) * 1024
            s_dbc = kb.scopy(arena[0:96, sd:sd + 512], ps[4][0:96, 0:512],
                             waits=[dn4])
            kb.consume_bank(4, "scalar")
            l_blk = kb.dma(dbc_in[d][c8].ap()[:, ts * TSZ:(ts + 1) * TSZ],
                           arena[0:96, sd:sd + 512], waits=[s_dbc], lane=2)
            stageslot_free[(ts + 1) % 2] = l_blk
            dbc_dmas.setdefault(c, []).append(l_blk)

        def stageA_tail(c):
            for db in range(2):
                base = R_UIN + db * 2052
                s_halo[0] = kb.scopy(arena[:, base + 1:base + 4],
                                     arena[:, base + 2049:base + 2052],
                                     waits=([v_conv0[c]] if db == 0 else ()))
            c8 = c % NCH
            c_ar_map[c] = kb.coll("AllReduce", ALU.add,
                                  [dbc_in[d][c8].ap().opt()],
                                  [dbc_out[d][c8].ap().opt()],
                                  waits=dbc_dmas[c])

        def bcast_src(c, n, which):
            row = 64 + n if which == "B" else 80 + n
            return dbc_out[d][c % NCH].ap()[row:row + 1, :].unsqueeze(0) \
                .broadcast_to([128, 1, TOK])

        def scan_prologue(c):
            nonlocal p_delta_prev
            st = {"v_scan": {}, "g_row": {}, "v_prod": {}, "l_bc": {}}
            scan_state[c] = st
            c_ar = c_ar_map[c]
            wdt_ = [c_ar] + [e for e in acc_free + stageslot_free
                             if e is not None]
            l_dt = kb.dma(arena[0:DTR, R_DBC:R_DBC + 2048],
                          dbc_out[d][c % NCH].ap()[0:DTR, :], waits=wdt_,
                          lane=3)
            wb = [c_ar]
            if (c - 1) in last_scan_v:
                wb.append(last_scan_v[c - 1])
            b1 = kb.dma(A(R_BB, 2048).rearrange("p (g n) -> p g n", g=1),
                        bcast_src(c, 0, "B"), waits=wb, lane=0)
            b2 = kb.dma(A(R_CB, 2048).rearrange("p (g n) -> p g n", g=1),
                        bcast_src(c, 0, "C"), waits=wb, lane=1)
            st["l_bc"][0] = [b1, b2]
            uw = list(u_out_dma[c])
            if (c - 1) in last_scan_v:
                uw.append(last_scan_v[c - 1])
            lu = [kb.dma(A(R_UW + db * 2048, 2048), usp_r[c % NSL][db], waits=uw,
                         lane=3) for db in range(2)]

            u_load_dma[c] = lu
            for m in range(2):
                for ns in range(4):
                    bank = m
                    dn = kb.mm_group(bank,
                                     [(misc[0:DTR, WDT + m * 128:WDT + (m + 1) * 128],
                                       arena[0:DTR, R_DBC + ns * 512:R_DBC + (ns + 1) * 512])],
                                     waits=[l_dt])
                    wv = [dn]
                    if m == 0 and ns == 0 and (c - 1) in last_scan_v:
                        wv.append(last_scan_v[c - 1])
                    kb.sact(A(R_DA + ns * 512, 512), ps[bank][:, 0:512],
                            AF.Sigmoid, bias=M(BDT + m),
                            scale=M(CNEG1), waits=wv)
                    kb.consume_bank(bank, "scalar")
                s_dl = None
                for ns in range(4):
                    s_dl = kb.sact(A(R_DL + m * 2048 + ns * 512, 512),
                                   A(R_DA + ns * 512, 512), AF.Ln)
                st.setdefault("s_dl", {})[m] = s_dl
            st["lu"] = lu
            p_delta_prev = now(P)

        def scan_prologue_b(c):
            st = scan_state[c]
            st_prev = scan_state.get(c - 1)
            st["v_w_db"] = {}
            for db in range(2):
                wv = [st["lu"][db], st["s_dl"][db]]
                if db == 0 and st_prev is not None:
                    wv.append(st_prev["v_yg"])
                st.setdefault("v_init_db", {})[db] = kb.vts(
                    A(R_YACC + db * 2048, 2048), A(R_UW + db * 2048, 2048),
                    M(DPC + db), ALU.mult, waits=wv)
                st["v_w_db"][db] = kb.vstt(A(R_UW + db * 2048, 2048),
                                           A(R_DL + db * 2048, 2048),
                                           -1.0, A(R_UW + db * 2048, 2048),
                                           ALU.mult, ALU.mult)
            st["v_w"] = now(V)
            v_w_last[0] = st["v_w"]

        def scan_rows(c, n_list):
            st = scan_state[c]
            v_scan, g_row, v_prod, l_bc = (st["v_scan"], st["g_row"],
                                           st["v_prod"], st["l_bc"])
            for n in n_list:
                buf = n % 2
                if n + 1 < DS and (n + 1) not in l_bc:
                    nbuf = (n + 1) % 2
                    ww = []
                    if n == 0 and last_prod31[0] is not None:
                        ww.append(last_prod31[0])
                    if n >= 1:
                        rp = 2 * n - 1
                        ww = [g_row[rp]]
                        ww.append(v_prod[rp])
                    b1 = kb.dma(A(R_BB + nbuf * 2048, 2048)
                                .rearrange("p (g nn) -> p g nn", g=1),
                                bcast_src(c, n + 1, "B"), waits=ww, lane=0)
                    b2 = kb.dma(A(R_CB + nbuf * 2048, 2048)
                                .rearrange("p (g nn) -> p g nn", g=1),
                                bcast_src(c, n + 1, "C"), waits=ww, lane=1)
                    l_bc[n + 1] = [b1, b2]
                for db in range(2):
                    row = n * 2 + db
                    ch = db * 16 + n
                    wda = [st["v_w"]] if row < 1 else [v_scan[row - 1]]
                    s_da = kb.sact(A(R_DA, 2048), A(R_DL + db * 2048, 2048),
                                   AF.Exp, scale=M(A_SB + ch), waits=wda)
                    kb.vtt(A(R_DBU, 2048), A(R_UW + db * 2048, 2048),
                           A(R_BB + buf * 2048, 2048), ALU.mult,
                           waits=l_bc[n])
                    v_scan[row] = kb.vscan(A(R_H, 2048), A(R_DA, 2048),
                                           A(R_DBU, 2048), M(HCAR + ch),
                                           waits=[s_da])
                    kb.vcopy(M(HCAR + ch), arena[:, R_H + 2047:R_H + 2048])
                    v_prod[row] = kb.vtt(A(R_PROD, 2048), A(R_H, 2048),
                                         A(R_CB + buf * 2048, 2048), ALU.mult,
                                         waits=l_bc[n])
                    g_row[row] = kb.vtt(A(R_YACC + db * 2048, 2048),
                                        A(R_YACC + db * 2048, 2048),
                                        A(R_PROD, 2048), ALU.add)
            if (DS - 1) in n_list:
                last_scan_v[c] = v_scan[2 * DS - 1]
                last_prod31[0] = v_prod[2 * DS - 1]

        def gate_q(c):
            nonlocal v_yg_prev, p_comp_prev
            st = scan_state[c]
            g_row, v_scan = st["g_row"], st["v_scan"]
            slot = c % NSL
            v_yg = None
            for db in range(2):
                zw = list(z_out_dma[c]) + [v_scan[2 * DS - 1]]
                if db == 1:
                    zw.append(v_yg)
                lz = kb.dma(A(R_DBU, 2048), zsp_r[slot][db], waits=zw, lane=3)
                z_load_dma.setdefault(c, []).append(lz)
                wv = [lz, g_row[2 * (DS - 1) + db]]
                if db == 0 and p_comp_prev:
                    wv.append(p_comp_prev)
                v_yg = kb.vtt(A(R_UWSTG + db * 2048, 2048),
                              A(R_YACC + db * 2048, 2048),
                              A(R_DBU, 2048), ALU.mult, waits=wv)
            st["v_yg"] = v_yg
            v_yg_prev = v_yg

        def composite_tasks(c):
            st = scan_state[c]
            for ts2 in range(NTS):
                for m in range(8):
                    yield (c, ts2, m, st)

        comp_bank = [0]
        ftile_free = [None, None]
        first_ft = [True]

        def emit_comp(task):
            nonlocal p_comp_prev
            c, ts2, m, st = task
            c8 = c % NCH
            bank = 5 + comp_bank[0] % 3
            comp_bank[0] += 1
            mats = [(A(R_WTS + kk * 1024 + m * 128, 128),
                     A(R_UWSTG + kk * 2048 + ts2 * TSZ, 512))
                    for kk in range(2)]
            dn = kb.mm_group(bank, mats, waits=[st["v_yg"]])
            cslot = (ts2 * 8 + m) % 2
            wv = [dn]
            if cstg_free[cslot]:
                wv.append(cstg_free[cslot])
            dst = arena[:, R_CSTG + cslot * 512:R_CSTG + cslot * 512 + 512]
            if rev:
                dst = dst[:, ::-1]
            if rev and c8 == NCH - 1:
                blk = 7 - c8
                sl = NTS - 1 - ts2
                fw = fwd_block_written[blk] + [e for e in (ftile_free[cslot],)
                                               if e is not None]
                if first_ft[0]:
                    fw = fw + list(stg_free[0])
                    first_ft[0] = False
                l_ft = kb.dma(arena[:, R_STG + cslot * 512:
                                    R_STG + cslot * 512 + 512],
                              rsin_r[sl][blk][m], waits=fw, lane=3)
                s_pe = kb.vtt(dst, ps[bank][:, 0:512],
                              arena[:, R_STG + cslot * 512:
                                    R_STG + cslot * 512 + 512][:, ::-1],
                              ALU.add, waits=wv + [l_ft])
                kb.consume_bank(bank, "vector")
                ftile_free[cslot] = s_pe
                dma_t = kb.dma(
                    rsin_r[sl][blk][m],
                    arena[:, R_CSTG + cslot * 512:R_CSTG + cslot * 512 + 512],
                    waits=[s_pe], lane=3)
                if m == 7:
                    c_rs4[sl] = kb.coll(
                        "ReduceScatter", ALU.add, [rs_in[sl].ap().opt()],
                        [rs_out[sl].ap().opt()],
                        waits=[dma_t, (GD, GD.ctr)])
            else:
                s_pe = kb.scopy(dst, ps[bank][:, 0:512], waits=wv)
                kb.consume_bank(bank, "scalar")
                if not rev:
                    sl = ts2
                    dma_t = kb.dma(
                        rsin_r[sl][c8][m],
                        arena[:, R_CSTG + cslot * 512:R_CSTG + cslot * 512 + 512],
                        waits=[s_pe], lane=3)
                else:
                    blk = 7 - c8
                    sl = NTS - 1 - ts2
                    dma_t = kb.dma(
                        rsin_r[sl][blk][m],
                        arena[:, R_CSTG + cslot * 512:R_CSTG + cslot * 512 + 512],
                        waits=[s_pe] + fwd_block_written[blk],
                        eng="gpsimd", accum=True)
            cstg_free[cslot] = dma_t
            if ts2 == NTS - 1 and m == 7:
                if not rev:
                    fwd_block_written[c8] = kb.dma_all()
                p_comp_prev = now(P)
                del scan_state[c]

        ROW_SLOTS = {0: range(0, 2), 1: range(2, 5), 2: range(5, 8),
                     3: range(8, 11)}
        pending_comp = []

        def drain_comp(k):
            for _ in range(k):
                if pending_comp:
                    emit_comp(pending_comp.pop(0))

        def scan_rows_drip(c, n_list):
            for n in n_list:
                scan_rows(c, [n])
                drain_comp(2)

        for i in range(NCH + 2):
            sc = gi0 + i - 2
            if i >= 2:
                scan_prologue(sc)
                scan_prologue_b(sc)
            for ts in range(NTS):
                if i < NCH:
                    stageA_ts(gi0 + i, ts)
                if i >= 2:
                    scan_rows_drip(sc, range(3 * ts, 3 * ts + 3))
            if i < NCH:
                stageA_tail(gi0 + i)
            if i >= 2:
                scan_rows_drip(sc, range(12, DS))
                drain_comp(99)
                gate_q(sc)
                pending_comp = list(composite_tasks(sc))
        drain_comp(99)


    p3_bar = [now(S), now(V), now(P), now(G), (GD, GD.ctr)] + kb.dma_all()
    for k in range(8):
        kb.dma(A(R_WEFF + k * 1024, 1024), wsp_r[k],
               waits=(p3_bar if k < 4 else ()))
    l_weff2 = kb.dma_all()
    rs_r = [r.ap().rearrange("(k p) t -> k p t", p=128) for r in rs_out]
    out_r = out_t.rearrange("(k p) t -> k p t", p=128)
    P3X = (R_XT, R_CB)
    P3XN = (R_WIN, R_DA)
    P3RS = (R_UWSTG, R_H)
    h3 = {}
    TORD = list(range(NTS - 1, -1, -1))
    for it in range(NTS + 1):
        ts = TORD[it] if it < NTS else None
        pv = TORD[it - 1] if it >= 1 else None
        if ts is not None:
            p = it % 2
            cs = slice(ts * TSZ, (ts + 1) * TSZ)
            w = ([h3[TORD[it - 2]]["ln_p1"], h3[TORD[it - 2]]["l_outdma"]]
                 if it >= 2 else [])
            l_x1 = kb.dma(A(P3X[p], 4096),
                          (x1_r[:, :, cs]).transpose([1, 0, 2]), waits=w)
            wt = [c_rs4[ts]] + ([h3[TORD[it - 2]]["v_t"]] if it >= 2 else [])
            l_t = kb.dma(A(P3RS[p], 4096),
                         (rs_r[ts]).transpose([1, 0, 2]),
                         waits=wt)
            for pb in range(8):
                kb.vts(A(R_TMP + pb * 512, 512), A(P3RS[p] + pb * 512, 512),
                       M(BPROJ + pb), ALU.add, waits=([l_t] if pb == 0 else ()))
            v_t = now(V)
            evs_x2 = []
            for pb in range(8):
                ev = kb.vtt(A(P3X[p] + pb * 512, 512),
                            A(R_TMP + pb * 512, 512),
                            A(P3X[p] + pb * 512, 512), ALU.add,
                            waits=([l_x1] if pb == 0 else ()))
                evs_x2.append(ev)
            h3[ts] = {"l_x1": l_x1, "v_t": v_t, "evs_x2": evs_x2}
        if pv is not None:
            p = (it - 1) % 2
            cs = slice(pv * TSZ, (pv + 1) * TSZ)
            v_x3, _, pe_ffn = ffn_apply(P3XN[p], P3X[p], P3X[p],
                                        h3[pv]["v_xn3"], h3[pv]["evs_x2"][-1],
                                        stat_ready=l_weff2)
            l_outdma = kb.dma(out_r[:, :, cs].transpose([1, 0, 2]),
                              A(P3X[p], 4096), waits=[v_x3])
            h3[pv].update(pe_ffn=pe_ffn, l_outdma=l_outdma)
        if ts is not None:
            p = it % 2
            dg = [h3[TORD[it - 2]]["pe_ffn"]] if it >= 2 else None
            v_xn3, ln_p1 = layer_norm(P3X[p], P3XN[p], h3[ts]["evs_x2"],
                                      dst_guard=dg)
            h3[ts].update(v_xn3=v_xn3, ln_p1=ln_p1)
    final_waits = kb.dma_all()
    kb.op("sync", lambda e: [e.wait_ge(sm.h, v) for sm, v in final_waits],
          sem=kb.Ls[0], incn=0)

    return nc, kb


def _emit(nc, kb):
    import contextlib
    with contextlib.ExitStack() as st:
        for s in kb.sems:
            s.h = st.enter_context(nc.semaphore(s.name))
        blk = st.enter_context(nc.Block())
        blk.sync(lambda e: kb.emit(e, "sync"))
        blk.scalar(lambda e: kb.emit(e, "scalar"))
        blk.vector(lambda e: kb.emit(e, "vector"))
        blk.tensor(lambda e: kb.emit(e, "tensor"))
        blk.gpsimd(lambda e: kb.emit(e, "gpsimd"))
    return nc


def _host_inputs(c, a):
    t0, t1 = c * TOK, (c + 1) * TOK
    d0, d1 = c * DSL, (c + 1) * DSL
    ct = np.ascontiguousarray
    m = {}
    m["x_fm"] = ct(a["L0_attn_x"][0, t0:t1, :].T)
    for d, sfx in (("f", "_f"), ("b", "_b")):
        winT = a["L0_attn_win" + sfx].T
        m[f"winT_{d}"] = ct(np.concatenate(
            [winT[:, d0:d1], winT[:, DI + d0:DI + d1]], axis=1))
        m[f"wxT_{d}"] = ct(a["L0_attn_wx" + sfx].T[d0:d1, :])
        m[f"wdtT_{d}"] = ct(a["L0_attn_wdt" + sfx].T[:, d0:d1])
        m[f"wconv_{d}"] = ct(a["L0_attn_wconv" + sfx][d0:d1, 0, :]
                             .reshape(2, 128, 4).transpose(1, 0, 2).reshape(128, 8))
        m[f"bconv_{d}"] = ct(a["L0_attn_bconv" + sfx][d0:d1].reshape(2, 128).T)
        m[f"bdt_{d}"] = ct(a["L0_attn_bdt" + sfx][d0:d1].reshape(2, 128).T)
        m[f"alog_{d}"] = ct(a["L0_attn_Alog" + sfx][d0:d1]
                            .reshape(2, 128, DS).transpose(1, 0, 2).reshape(128, 32))
        m[f"dp_{d}"] = ct(a["L0_attn_Dp" + sfx][d0:d1].reshape(2, 128).T)
        m[f"woutc_{d}"] = ct(a["L0_attn_wout" + sfx][:, d0:d1])
        half = a["L0_attn_wproj"][:, :DM] if d == "f" else a["L0_attn_wproj"][:, DM:]
        m[f"wprojT_{d}"] = ct(half.T)
    m["w_T"] = ct(a["L0_ffn_W"].T)
    m["v_rT"] = ct(a["L0_ffn_V_r"].T)
    m["c_T"] = ct(a["L0_ffn_C"].T)
    m["b_ffn"] = ct(a["L0_ffn_b"].reshape(8, 128).T)
    m["bproj_v"] = ct(a["L0_attn_bproj"].reshape(8, 128).T)
    return m


def kernel(**inputs):
    a = {k: np.asarray(v, dtype=np.float32) for k, v in inputs.items()}
    nc, kb = build_program()
    _emit(nc, kb)
    in_maps = [_host_inputs(c, a) for c in range(NCORES)]
    res = run_bass_kernel_spmd(nc, in_maps, core_ids=list(range(NCORES)))
    outs = [np.asarray(res.results[c]["out"]).T for c in range(NCORES)]
    return np.concatenate(outs, axis=0).reshape(1, SEQ, DM)
